# version 05
# baseline (speedup 1.0000x reference)
import numpy as np
from contextlib import ExitStack

import concourse.bass as bass
import concourse.mybir as mybir
from concourse.bass_utils import run_bass_kernel_spmd

F32 = mybir.dt.float32
BF16 = mybir.dt.bfloat16
I32 = mybir.dt.int32
AF = mybir.ActivationFunctionType
ALU = mybir.AluOpType
AX = mybir.AxisListType

D = 2048
H = 16
TOK = 512
NWIN = 4
INTER = 5504
NIC = 43
NEG = -3.4028235e38
SCALE = float(128.0) ** -0.5
LN2PI_INV = float(1.0 / (2.0 * np.pi))
CW1 = 6.28125
CW2 = float(np.float32(2.0 * np.pi - 6.28125))
CW3 = float(2.0 * np.pi - CW1 - float(np.float32(CW2)))
LN1E4_64 = float(np.log(10000.0) / 64.0)
NCORE = 8
STAGE = 99


class _Stub:
    __slots__ = ("incs",)

    def __init__(self):
        self.incs = []

    def then_inc(self, sem, n=1):
        self.incs.append((sem, n))
        return self


class _Rec:

    def __init__(self):
        self.log = []

    def __getattr__(self, name):
        def call(*a, **k):
            stub = _Stub()
            self.log.append((name, a, k, stub))
            return stub
        return call

    def replay(self, eng):
        for name, a, k, stub in self.log:
            instr = getattr(eng, name)(*a, **k)
            for sem, n in stub.incs:
                instr.then_inc(sem, n)


class _EmitShim:
    def __init__(self):
        self.recs = {n: _Rec() for n in ("sync", "vector", "scalar", "tensor",
                                         "gpsimd")}

    def _mk(self, n):
        return lambda f: f(self.recs[n])

    def __getattr__(self, n):
        if n in ("sync", "vector", "scalar", "tensor", "gpsimd"):
            return self._mk(n)
        raise AttributeError(n)
DMASCRATCH = 16384
GHOST = 1
GSH = D // NCORE
DSH = INTER // NCORE


def build_program():
    nc = bass.Bass("TRN2", target_bir_lowering=False,
                   dynamic_dma_scratch_size=DMASCRATCH,
                   detect_race_conditions=False)

    d_x = nc.dram_tensor("x", [2048, D], F32, kind="ExternalInput").ap()
    d_wqkvT = nc.dram_tensor("wqkvT", [D, 3 * D], F32, kind="ExternalInput").ap()
    d_woutT = nc.dram_tensor("woutT", [D, D], F32, kind="ExternalInput").ap()
    d_normw = nc.dram_tensor("normw", [1, D], F32, kind="ExternalInput").ap()
    d_pmk = nc.dram_tensor("pmk", [128, 512], F32, kind="ExternalInput").ap()
    d_pmv = nc.dram_tensor("pmv", [16, 2048], F32, kind="ExternalInput").ap()
    d_lut = nc.dram_tensor("lut", [1, 16], F32, kind="ExternalInput").ap()
    d_gidx = nc.dram_tensor("gidx", [GSH, INTER], I32, kind="ExternalInput").ap()
    d_uidx = nc.dram_tensor("uidx", [GSH, INTER], I32, kind="ExternalInput").ap()
    d_didx = nc.dram_tensor("didx", [DSH, D], I32, kind="ExternalInput").ap()
    d_y = nc.dram_tensor("y", [2048, D], F32, kind="ExternalOutput").ap()

    cc_in_g = nc.dram_tensor("cc_in_g", [GSH, INTER], BF16, kind="Internal").ap()
    cc_in_u = nc.dram_tensor("cc_in_u", [GSH, INTER], BF16, kind="Internal").ap()
    cc_in_d = nc.dram_tensor("cc_in_d", [DSH, D], BF16, kind="Internal").ap()
    WgT_d = nc.dram_tensor("WgT_d", [D, INTER], BF16, kind="Internal").ap()
    WuT_d = nc.dram_tensor("WuT_d", [D, INTER], BF16, kind="Internal").ap()
    WdT_d = nc.dram_tensor("WdT_d", [INTER, D], BF16, kind="Internal").ap()
    WgT_v = WgT_d.rearrange("(dc p) i -> p dc i", p=128)
    WuT_v = WuT_d.rearrange("(dc p) i -> p dc i", p=128)
    WdT_v = WdT_d.rearrange("(ic p) o -> p ic o", p=128)
    wqkvT_v = d_wqkvT.rearrange("(dc p) o -> p dc o", p=128)
    woutT_v = d_woutT.rearrange("(dc p) o -> p dc o", p=128)

    ctx = ExitStack()
    with ctx:
        sbt = lambda name, shape, dt: ctx.enter_context(nc.sbuf_tensor(name, shape, dt))
        pst = lambda name, shape: ctx.enter_context(nc.psum_tensor(name, shape, F32))

        x_sb = [sbt(f"x{i}", [128, D], F32) for i in range(4)]
        normw_b = sbt("normw_b", [128, D], BF16)
        lutb = sbt("lutb", [128, 16], F32)
        Btri = sbt("Btri", [128, 128], F32)
        B0 = sbt("B0", [128, 128], F32)
        I128 = sbt("I128", [128, 128], BF16)
        pmkP = sbt("pmkP", [128, 512], BF16)
        pmv_t = sbt("pmv_t", [16, 2048], BF16)
        pio = sbt("pio", [128, 1], F32)
        invf2 = sbt("invf2", [128, 1], F32)
        kvec = sbt("kvec", [128, 8], F32)
        ones_row = sbt("ones_row", [1, 128], F32)
        lut_row = sbt("lut_row", [1, 16], F32)
        normw_row = sbt("normw_row", [1, D], F32)
        Ctab = sbt("Ctab", [128, TOK], F32)
        Stab = sbt("Stab", [128, TOK], F32)
        ropet1 = sbt("ropet1", [128, TOK], F32)
        ropet2 = sbt("ropet2", [128, TOK], F32)
        stat = sbt("stat", [128, 48], F32)
        xn_sb = sbt("xn_sb", [128, D], BF16)
        xn2_sb = sbt("xn2_sb", [128, D], BF16)
        xnT = sbt("xnT", [128, 16, TOK], BF16)
        qh = [sbt(f"qh{i}", [128, TOK], BF16) for i in range(2)]
        kh = [sbt(f"kh{i}", [128, TOK], BF16) for i in range(2)]
        v_sb = [sbt(f"v{i}", [128, D], BF16) for i in range(4)]
        attn_sb = sbt("attn_sb", [128, 16, TOK], BF16)
        P_sb = [sbt(f"P{i}", [128, 544], BF16) for i in range(2)]
        pmPT = sbt("pmPT", [16, TOK], BF16)
        PT_t = sbt("PT_t", [128, 4, TOK], BF16)
        sd_sb = sbt("sd_sb", [128, 128], F32)
        silu_sb = sbt("silu_sb", [128, TOK], F32)
        rope_f = silu_sb
        h_sb = sbt("h_sb", [128, NIC, TOK], BF16) if (STAGE > 5 and not 30 <= STAGE < 40) else None
        qkw = [sbt(f"qkw{i}", [128, 16, 128], BF16) for i in range(2)]
        bigw = [sbt(f"bigw{i}", [128, 16, 256], BF16) for i in range(2)]
        gidx_b = [sbt(f"gidx{i}", [128, 688], I32) for i in range(2)]
        gbf = sbt("gbf", [128, 688], BF16)
        gmask = sbt("gmask", [128, 688], BF16)
        gacc = [sbt(f"gacc{i}", [128, 688], BF16) for i in range(2)]

        bigw_d = [b.ap().rearrange("p a b -> p (a b)").rearrange(
            "p (ic o) -> p ic o", o=1024) for b in bigw]

        pbank = [pst(f"pb{i}", [128, 512]) for i in range(8)]

        sems = {}
        for s_ in ["pe", "dve", "act", "pool", "cc",
                   "d_x", "d_out", "d_qk0", "d_qk1", "d_vw0", "d_vw1",
                   "d_g0", "d_g1", "d_idx", "d_sh", "d_c", "d_c2"]:
            sems[s_] = ctx.enter_context(nc.semaphore(s_))
        block = _EmitShim()

        ms = {"pe": 0, "dve": 0, "act": 0, "pool": 0, "cc": 0}
        dma = {k: 0 for k in ["d_x", "d_out", "d_qk0", "d_qk1", "d_vw0",
                              "d_vw1", "d_g0", "d_g1", "d_idx", "d_sh", "d_c",
                              "d_c2"]}
        bank_guard = [None] * 8
        W = {}

        def minc(key, instr):
            ms[key] += 1
            instr.then_inc(sems[key], 1)
            return ms[key]

        def dinc(key, instr):
            dma[key] += 16
            instr.then_inc(sems[key], 16)
            return dma[key]

        def wait_bank(eng, i):
            g = bank_guard[i]
            if g is not None:
                eng.wait_ge(sems[g[0]], g[1])
                bank_guard[i] = None

        def set_guard(i, key, val):
            bank_guard[i] = (key, val)

        def hop(key, eng, instr):
            v = minc(key, instr)
            eng.wait_ge(sems[key], v)

        def c_load(eng):
            dinc("d_c", eng.dma_start(out=normw_row[:, :], in_=d_normw))
            dinc("d_c", eng.dma_start(out=lut_row[:, :], in_=d_lut))
        block.sync(c_load)

        def c_load2(eng):
            dinc("d_c2", eng.dma_start(out=pmkP[:, :], in_=d_pmk))
            dinc("d_c2", eng.dma_start(out=pmv_t[:, :], in_=d_pmv))
        block.gpsimd(c_load2)
        W["c_dmas"] = dma["d_c"]
        W["c2_dmas"] = dma["d_c2"]

        def c_pool(eng):
            eng.iota(pio[:, 0:1], pattern=[[0, 1]], base=0, channel_multiplier=1,
                     allow_small_or_imprecise_dtypes=True)
            eng.iota(Btri[:, :], pattern=[[1, 128]], base=0, channel_multiplier=0,
                     allow_small_or_imprecise_dtypes=True)
            eng.memset(kvec[:, 0:1], 1e-5)
            eng.memset(kvec[:, 1:2], 1e-6)
            eng.memset(kvec[:, 2:3], float(np.pi / 2))
            i = eng.memset(ones_row[:, :], 1.0)
            minc("pool", i)
        block.gpsimd(c_pool)

        def c_dve(eng):
            eng.wait_ge(sems["pool"], 1)
            eng.tensor_scalar(out=I128[:, :], in0=Btri[:, :], scalar1=pio[:, 0:1],
                              scalar2=None, op0=ALU.is_equal)
            i = eng.tensor_scalar(out=B0[:, :], in0=Btri[:, :], scalar1=pio[:, 0:1],
                                  scalar2=NEG, op0=ALU.is_gt, op1=ALU.mult)
            hop("dve", eng, i)
            i = eng.tensor_copy(out=Btri[:, :], in_=B0[:, :])
            hop("dve", eng, i)
            eng.memset(B0[:, 0:16], 0.0)
            i = eng.tensor_scalar(out=invf2[:, 0:1], in0=pio[:, 0:1], scalar1=64.0,
                                  scalar2=64.0, op0=ALU.is_ge, op1=ALU.mult)
            hop("dve", eng, i)
            i = eng.tensor_sub(out=invf2[:, 0:1], in0=pio[:, 0:1], in1=invf2[:, 0:1])
            minc("dve", i)
        block.vector(c_dve)

        def c_act(eng):
            eng.wait_ge(sems["dve"], ms["dve"])
            i = eng.activation(out=invf2[:, 0:1], in_=invf2[:, 0:1], func=AF.Exp,
                               scale=-LN1E4_64)
            minc("act", i)
        block.scalar(c_act)

        def c_pe(eng):
            eng.wait_ge(sems["d_c"], W["c_dmas"])
            eng.wait_ge(sems["pool"], 1)
            eng.matmul(pbank[0][:, 0:16], ones_row[:, :], lut_row[:, :],
                       start=True, stop=True)
            for j in range(4):
                i = eng.matmul(pbank[1 + j][:, :], ones_row[:, :],
                               normw_row[:, j * 512:(j + 1) * 512],
                               start=True, stop=True)
            minc("pe", i)
        block.tensor(c_pe)
        W["pe_bcast"] = ms["pe"]

        def c_dve2(eng):
            eng.wait_ge(sems["pe"], W["pe_bcast"])
            eng.tensor_copy(out=lutb[:, :], in_=pbank[0][:, 0:16])
            for j in range(4):
                i = eng.tensor_copy(out=normw_b[:, j * 512:(j + 1) * 512],
                                    in_=pbank[1 + j][:, :])
            minc("dve", i)
        block.vector(c_dve2)
        W["consts_done"] = ms["dve"]
        for i in range(5):
            set_guard(i, "dve", W["consts_done"])

        ghost_tasks = []
        for (src, dst) in ((d_gidx, cc_in_g), (d_uidx, cc_in_u)):
            for rb in range(2):
                for cb in range(8):
                    ghost_tasks.append(
                        (src[rb * 128:(rb + 1) * 128, cb * 688:(cb + 1) * 688],
                         dst[rb * 128:(rb + 1) * 128, cb * 688:(cb + 1) * 688],
                         128, 688))
        for rb in range(6):
            rows = 128 if rb < 5 else 48
            for cb in range(4):
                ghost_tasks.append(
                    (d_didx[rb * 128:rb * 128 + rows, cb * 512:(cb + 1) * 512],
                     cc_in_d[rb * 128:rb * 128 + rows, cb * 512:(cb + 1) * 512],
                     rows, 512))
        n_ghost = len(ghost_tasks)
        gstate = {"next": 0}

        def ghost_load(eng, k):
            src, dst, rows, width = ghost_tasks[k]
            if k >= 2:
                eng.wait_ge(sems["dve"], W[f"ghost_dve_{k-2}"])
            dinc("d_idx", eng.dma_start(out=gidx_b[k % 2][0:rows, 0:width], in_=src))
            W[f"ghost_ld_{k}"] = dma["d_idx"]

        def ghost_dve(eng, k):
            src, dst, rows, width = ghost_tasks[k]
            eng.wait_ge(sems["d_idx"], W[f"ghost_ld_{k}"])
            if k >= 2:
                eng.wait_ge(sems["d_sh"], W[f"ghost_st_{k-2}"])
            g = gacc[k % 2]
            eng.tensor_copy(out=gbf[0:rows, 0:width], in_=gidx_b[k % 2][0:rows, 0:width])
            eng.tensor_scalar(out=g[0:rows, 0:width], in0=gbf[0:rows, 0:width],
                              scalar1=0.0, scalar2=lutb[0:rows, 0:1],
                              op0=ALU.is_equal, op1=ALU.mult)
            i = None
            for j in range(1, 16):
                eng.tensor_scalar(out=gmask[0:rows, 0:width], in0=gbf[0:rows, 0:width],
                                  scalar1=float(j), scalar2=None, op0=ALU.is_equal)
                i = eng.scalar_tensor_tensor(out=g[0:rows, 0:width],
                                             in0=gmask[0:rows, 0:width],
                                             scalar=lutb[0:rows, j:j + 1],
                                             in1=g[0:rows, 0:width],
                                             op0=ALU.mult, op1=ALU.add)
            W[f"ghost_dve_{k}"] = minc("dve", i)

        def ghost_store(eng, k):
            src, dst, rows, width = ghost_tasks[k]
            eng.wait_ge(sems["dve"], W[f"ghost_dve_{k}"])
            dinc("d_sh", eng.dma_start(out=dst, in_=gacc[k % 2][0:rows, 0:width]))
            W[f"ghost_st_{k}"] = dma["d_sh"]
            if k + 1 == 16:
                W["cc_rdy_0"] = dma["d_sh"]
            if k + 1 == 32:
                W["cc_rdy_1"] = dma["d_sh"]
            if k + 1 == 56:
                W["cc_rdy_2"] = dma["d_sh"]

        def ghost_tick(n):
            if not GHOST:
                return
            lim = min(gstate["next"] + n, n_ghost)
            while gstate["next"] < lim:
                k = gstate["next"]
                block.sync(lambda e, k=k: ghost_load(e, k))
                block.vector(lambda e, k=k: ghost_dve(e, k))
                block.sync(lambda e, k=k: ghost_store(e, k))
                gstate["next"] = k + 1

        def ghost_collective(eng, mat):
            eng.wait_ge(sems["d_sh"], W[f"cc_rdy_{mat}"])
            io = [(cc_in_g, WgT_d), (cc_in_u, WuT_d), (cc_in_d, WdT_d)][mat]
            i = eng.collective_compute(
                "AllGather", ALU.bypass,
                replica_groups=[list(range(NCORE))],
                ins=[io[0].opt()], outs=[io[1].opt()])
            ms["cc"] += 1
            i.then_inc(sems["cc"], 1)
            eng.wait_ge(sems["cc"], ms["cc"])

        def x_load(eng, w, tc):
            if w > 0 and f"store_{w-1}_{tc}" in W:
                eng.wait_ge(sems["d_out"], W[f"store_{w-1}_{tc}"])
            dinc("d_x", eng.dma_start(
                out=x_sb[tc][:, :],
                in_=d_x[w * TOK + tc * 128: w * TOK + (tc + 1) * 128, :]))
            W[f"xload_{w}_{tc}"] = dma["d_x"]

        def rope_build_pool(eng, w):
            if w > 0 and STAGE > 5:
                eng.wait_ge(sems["dve"], W[f"h_{w-1}_{NIC-1}"])
            i = eng.iota(rope_f[:, :], pattern=[[1, TOK]], base=w * TOK,
                         channel_multiplier=0, allow_small_or_imprecise_dtypes=True)
            W[f"riota_{w}"] = minc("pool", i)

        def rope_build_dve(eng, w):
            eng.wait_ge(sems["pool"], W[f"riota_{w}"])
            eng.wait_ge(sems["act"], 1)
            eng.tensor_scalar(out=rope_f[:, :], in0=rope_f[:, :],
                              scalar1=invf2[:, 0:1], scalar2=None, op0=ALU.mult)
            eng.tensor_scalar(out=ropet1[:, :], in0=rope_f[:, :],
                              scalar1=LN2PI_INV, scalar2=None, op0=ALU.mult)
            rope_i = ropet2.ap().bitcast(I32)
            eng.tensor_copy(out=rope_i, in_=ropet1[:, :])
            eng.tensor_copy(out=ropet1[:, :], in_=rope_i)
            eng.scalar_tensor_tensor(out=rope_f[:, :], in0=ropet1[:, :], scalar=-CW1,
                                     in1=rope_f[:, :], op0=ALU.mult, op1=ALU.add)
            eng.scalar_tensor_tensor(out=rope_f[:, :], in0=ropet1[:, :], scalar=-CW2,
                                     in1=rope_f[:, :], op0=ALU.mult, op1=ALU.add)
            eng.scalar_tensor_tensor(out=rope_f[:, :], in0=ropet1[:, :], scalar=-CW3,
                                      in1=rope_f[:, :], op0=ALU.mult, op1=ALU.add)
            PI = float(np.pi); TWO_PI = float(2.0 * np.pi)
            eng.tensor_scalar(out=ropet1[:, :], in0=rope_f[:, :], scalar1=PI,
                              scalar2=TWO_PI, op0=ALU.is_gt, op1=ALU.mult)
            eng.tensor_sub(out=rope_f[:, :], in0=rope_f[:, :], in1=ropet1[:, :])
            eng.tensor_scalar(out=ropet1[:, :], in0=rope_f[:, :], scalar1=-PI,
                              scalar2=TWO_PI, op0=ALU.is_lt, op1=ALU.mult)
            i = eng.tensor_add(out=rope_f[:, :], in0=rope_f[:, :], in1=ropet1[:, :])
            W[f"rred_{w}"] = minc("dve", i)

        def rope_build_act(eng, w):
            eng.wait_ge(sems["dve"], W[f"rred_{w}"])
            eng.activation(out=Stab[:, :], in_=rope_f[:, :], func=AF.Sin)
            eng.activation(out=ropet1[:, :], in_=rope_f[:, :], func=AF.Abs)
            i = eng.activation(out=Ctab[:, :], in_=ropet1[:, :], func=AF.Sin,
                               scale=-1.0, bias=kvec[:, 2:3])
            W[f"rtab_{w}"] = minc("act", i)

        def ln_stats_act(eng, w, tc, tag):
            if tag == "ln1":
                eng.wait_ge(sems["d_x"], W[f"xload_{w}_{tc}"])
            else:
                eng.wait_ge(sems["dve"], W[f"worid_{w}_7_{tc}"])
            eng.activation(out=xn_sb[:, :], in_=x_sb[tc][:, :], func=AF.Square,
                           accum_out=stat[:, 4 + tc:5 + tc])
            i = eng.activation(out=xn_sb[:, :], in_=x_sb[tc][:, :], func=AF.Copy,
                               accum_out=stat[:, tc:1 + tc])
            return i

        def ln_var_dve(eng, w, tag):
            eng.wait_ge(sems["act"], W[f"{tag}st_{w}"])
            sm = stat[:, 0:4]; sq = stat[:, 4:8]
            m = stat[:, 16:20]; var = stat[:, 20:24]; mm = stat[:, 24:28]
            i = eng.tensor_scalar(out=m, in0=sm, scalar1=1.0 / D, scalar2=None,
                                  op0=ALU.mult)
            hop("dve", eng, i)
            eng.tensor_mul(out=mm, in0=m, in1=m)
            i = eng.tensor_scalar(out=var, in0=sq, scalar1=1.0 / D, scalar2=None,
                                  op0=ALU.mult)
            hop("dve", eng, i)
            i = eng.tensor_sub(out=var, in0=var, in1=mm)
            W[f"{tag}var_{w}"] = minc("dve", i)

        def ln_rstd_act(eng, w, tag):
            eng.wait_ge(sems["dve"], W[f"{tag}var_{w}"])
            i = eng.activation(out=stat[:, 20:24], in_=stat[:, 20:24], func=AF.Ln,
                               bias=kvec[:, 0:1])
            hop("act", eng, i)
            i = eng.activation(out=stat[:, 20:24], in_=stat[:, 20:24], func=AF.Exp,
                               scale=-0.5)
            W[f"{tag}rstd_{w}"] = minc("act", i)

        def ln1_rms_dve(eng, w):
            eng.wait_ge(sems["act"], W[f"ln1rstd_{w}"])
            sm = stat[:, 0:4]; sq = stat[:, 4:8]; m = stat[:, 16:20]
            rstd = stat[:, 20:24]; t = stat[:, 24:28]; sy = stat[:, 28:32]
            i = eng.tensor_mul(out=t, in0=sm, in1=m)
            hop("dve", eng, i)
            eng.tensor_sub(out=t, in0=sq, in1=t)
            i = eng.tensor_mul(out=sy, in0=rstd, in1=rstd)
            hop("dve", eng, i)
            i = eng.tensor_mul(out=sy, in0=sy, in1=t)
            W[f"ln1sy_{w}"] = minc("dve", i)

        def ln1_rms_act(eng, w):
            eng.wait_ge(sems["dve"], W[f"ln1sy_{w}"])
            i = eng.activation(out=stat[:, 28:32], in_=stat[:, 28:32], func=AF.Ln,
                               scale=1.0 / D, bias=kvec[:, 1:2])
            hop("act", eng, i)
            i = eng.activation(out=stat[:, 28:32], in_=stat[:, 28:32], func=AF.Exp,
                               scale=-0.5)
            W[f"ln1rms_{w}"] = minc("act", i)

        def ln1_ab_dve(eng, w):
            eng.wait_ge(sems["act"], W[f"ln1rms_{w}"])
            a = stat[:, 8:12]; b = stat[:, 12:16]
            i = eng.tensor_mul(out=a, in0=stat[:, 20:24], in1=stat[:, 28:32])
            hop("dve", eng, i)
            i = eng.tensor_mul(out=b, in0=stat[:, 16:20], in1=a)
            hop("dve", eng, i)
            i = eng.tensor_scalar(out=b, in0=b, scalar1=-1.0, scalar2=None, op0=ALU.mult)
            hop("dve", eng, i)
            W[f"ln1ab_{w}"] = ms["dve"]

        def ln2_ab_dve(eng, w):
            eng.wait_ge(sems["act"], W[f"ln2rstd_{w}"])
            a = stat[:, 8:12]; b = stat[:, 12:16]
            i = eng.tensor_copy(out=a, in_=stat[:, 20:24])
            hop("dve", eng, i)
            i = eng.tensor_mul(out=b, in0=stat[:, 16:20], in1=a)
            hop("dve", eng, i)
            i = eng.tensor_scalar(out=b, in0=b, scalar1=-1.0, scalar2=None, op0=ALU.mult)
            hop("dve", eng, i)
            W[f"ln2ab_{w}"] = ms["dve"]

        def xstage_dve(eng, w, tc, tag):
            if tc > 0:
                eng.wait_ge(sems["pe"], W[f"xTmm_{w}_{tc-1}_3"])
            a = stat[:, 8 + tc:9 + tc]
            b = stat[:, 12 + tc:13 + tc]
            if tag == "xn":
                eng.tensor_scalar(out=xn2_sb[:, :], in0=x_sb[tc][:, :], scalar1=a,
                                  scalar2=b, op0=ALU.mult, op1=ALU.add)
                i = eng.tensor_mul(out=xn_sb[:, :], in0=xn2_sb[:, :], in1=normw_b[:, :])
            else:
                i = eng.tensor_scalar(out=xn_sb[:, :], in0=x_sb[tc][:, :], scalar1=a,
                                      scalar2=b, op0=ALU.mult, op1=ALU.add)
            W[f"stage_{w}_{tc}"] = minc("dve", i)

        def xT_pe_grp(eng, w, tc, grp):
            if grp == 0:
                eng.wait_ge(sems["dve"], W[f"stage_{w}_{tc}"])
            bk = grp % 2
            wait_bank(eng, bk)
            i = None
            for j in range(4):
                dc = grp * 4 + j
                i = eng.matmul(pbank[bk][:, j * 128:(j + 1) * 128],
                               xn_sb[:, dc * 128:(dc + 1) * 128], I128[:, :],
                               start=True, stop=True)
            W[f"xTmm_{w}_{tc}_{grp}"] = minc("pe", i)

        def xT_evac_grp(eng, w, tc, grp):
            bk = grp % 2
            eng.wait_ge(sems["pe"], W[f"xTmm_{w}_{tc}_{grp}"])
            dst = xnT.ap()[:, grp * 4:(grp + 1) * 4, tc * 128:(tc + 1) * 128]
            src = pbank[bk].ap().rearrange("p (j f) -> p j f", j=4)
            i = eng.activation(out=dst, in_=src, func=AF.Copy)
            minc("act", i)
            set_guard(bk, "act", ms["act"])
            W[f"xT_{w}_{tc}"] = ms["act"]

        def qk_wload(eng, w, hp, which):
            idx = hp * 4 + which
            par = idx % 2
            tile = (hp * 2 + (which & 1)) if which < 2 else (16 + hp * 2 + (which & 1))
            if idx >= 2:
                eng.wait_ge(sems["pe"], W[f"qkuse_{w}_{idx-2}"])
            elif w > 0:
                eng.wait_ge(sems["pe"], W[f"guuse_{w-1}_{idx}_42"])
            dinc(f"d_qk{par}", eng.dma_start(
                out=qkw[par][:, :, :],
                in_=wqkvT_v[:, :, tile * 128:(tile + 1) * 128]))
            W[f"qkload_{w}_{idx}"] = dma[f"d_qk{par}"]

        def qk_mm(eng, w, hp, which):
            idx = hp * 4 + which
            par = idx % 2
            bank = 2 + which
            eng.wait_ge(sems[f"d_qk{par}"], W[f"qkload_{w}_{idx}"])
            if hp == 0 and which == 0:
                eng.wait_ge(sems["act"], W[f"xT_{w}_3"])
            wait_bank(eng, bank)
            for dc in range(16):
                i = eng.matmul(pbank[bank][:, :], qkw[par][:, dc, :],
                               xnT[:, dc, :], start=(dc == 0), stop=(dc == 15))
            W[f"qkuse_{w}_{idx}"] = minc("pe", i)

        def rope_pair_dve(eng, w, hp):
            eng.wait_ge(sems["pe"], W[f"qkuse_{w}_{hp*4+3}"])
            eng.wait_ge(sems["act"], W[f"rtab_{w}"])
            qE, qO = qh[0], qh[1]
            kE, kO = kh[0], kh[1]
            i = None
            for (E, O, bE, bO) in ((qE, qO, 2, 3), (kE, kO, 4, 5)):
                eng.tensor_mul(out=ropet1[:, :], in0=pbank[bE][:, :], in1=Ctab[:, :])
                eng.tensor_mul(out=ropet2[:, :], in0=pbank[bO][:, :], in1=Stab[:, :])
                eng.tensor_sub(out=E[:, :], in0=ropet1[:, :], in1=ropet2[:, :])
                eng.tensor_mul(out=ropet1[:, :], in0=pbank[bO][:, :], in1=Ctab[:, :])
                eng.tensor_mul(out=ropet2[:, :], in0=pbank[bE][:, :], in1=Stab[:, :])
                i = eng.tensor_add(out=O[:, :], in0=ropet1[:, :], in1=ropet2[:, :])
            W[f"rope_{w}_{hp}"] = minc("dve", i)
            for b in (2, 3, 4, 5):
                set_guard(b, "dve", ms["dve"])

        def v_wload(eng, w, vs):
            par = vs % 2
            if vs >= 2:
                eng.wait_ge(sems["pe"], W[f"vuse_{w}_{vs-2}"])
            elif w > 0:
                eng.wait_ge(sems["pe"], W[f"bigw_free_{w}_{par}"])
            dinc(f"d_vw{par}", eng.dma_start(
                out=bigw[par][:, :, :],
                in_=wqkvT_v[:, :, 4096 + vs * 256: 4096 + (vs + 1) * 256]))
            W[f"vload_{w}_{vs}"] = dma[f"d_vw{par}"]

        def v_mm_tc(eng, w, vs, tc):
            par = vs % 2
            if tc == 0:
                eng.wait_ge(sems[f"d_vw{par}"], W[f"vload_{w}_{vs}"])
            bank = 6 + tc % 2
            wait_bank(eng, bank)
            i = None
            for dc in range(16):
                i = eng.matmul(pbank[bank][:, 0:256],
                               xnT[:, dc, tc * 128:(tc + 1) * 128],
                               bigw[par][:, dc, :], start=(dc == 0), stop=(dc == 15))
            W[f"vmm_{w}_{vs}_{tc}"] = minc("pe", i)
            if tc == 3:
                W[f"vuse_{w}_{vs}"] = ms["pe"]

        def v_evac_tc(eng, w, vs, tc):
            bank = 6 + tc % 2
            eng.wait_ge(sems["pe"], W[f"vmm_{w}_{vs}_{tc}"])
            i = eng.activation(out=v_sb[tc][:, vs * 256:(vs + 1) * 256],
                               in_=pbank[bank][:, 0:256], func=AF.Copy)
            minc("act", i)
            set_guard(bank, "act", ms["act"])
            W[f"vev_{w}_{vs}"] = ms["act"]

        SC_LAYOUT = {0: (2, 16, 2, 16), 1: (3, 144, 3, 144),
                     2: (4, 272, 4, 272), 3: (5, 400, 2, 0)}

        def sc_mm(eng, w, h, Qc):
            hp, half = h // 2, h % 2
            lo = half * 64
            nb, nw, db, doff = SC_LAYOUT[Qc]
            if Qc == 0 and half == 0:
                eng.wait_ge(sems["dve"], W[f"rope_{w}_{hp}"])
            if w == 0 and h == 0 and Qc == 0:
                eng.wait_ge(sems["d_c2"], W["c2_dmas"])
            wait_bank(eng, nb)
            if db != nb:
                wait_bank(eng, db)
            qE, qO = qh[0], qh[1]
            kE, kO = kh[0], kh[1]
            qEs = qE[lo:lo + 64, Qc * 128:(Qc + 1) * 128]
            qOs = qO[lo:lo + 64, Qc * 128:(Qc + 1) * 128]
            eng.matmul(pbank[nb][:, 0:16], qEs,
                       pmkP[lo:lo + 64, h * 16:(h + 1) * 16],
                       start=True, stop=False)
            i = eng.matmul(pbank[nb][:, 0:16], qOs,
                           pmkP[lo:lo + 64, 256 + h * 16:256 + (h + 1) * 16],
                           start=False, stop=True)
            for tj in range(Qc + 1):
                if tj < Qc or db == nb:
                    out = pbank[nb][:, 16 + tj * 128:16 + (tj + 1) * 128]
                else:
                    out = pbank[db][:, doff:doff + 128]
                eng.matmul(out, qEs, kE[lo:lo + 64, tj * 128:(tj + 1) * 128],
                           start=True, stop=False)
                i = eng.matmul(out, qOs, kO[lo:lo + 64, tj * 128:(tj + 1) * 128],
                               start=False, stop=True)
            W[f"sc_{w}_{h}_{Qc}"] = minc("pe", i)
            if Qc == 3 and half == 1:
                W[f"qk_consumed_{w}_{hp}"] = ms["pe"]

        def soft_dve(eng, w, h, Qc):
            nb, nw, db, doff = SC_LAYOUT[Qc]
            eng.wait_ge(sems["pe"], W[f"sc_{w}_{h}_{Qc}"])
            diag = pbank[db][:, doff:doff + 128]
            Bt = B0 if Qc == 0 else Btri
            eng.tensor_add(out=sd_sb[:, :], in0=diag, in1=Bt[:, :])
            mx = stat[:, 40:41]; r1 = stat[:, 41:42]
            eng.reduce_max(out=mx, in_=pbank[nb][:, 0:nw], axis=AX.X)
            i = eng.reduce_max(out=r1, in_=sd_sb[:, :], axis=AX.X)
            hop("dve", eng, i)
            i = eng.tensor_max(out=mx, in0=mx, in1=r1)
            hop("dve", eng, i)
            i = eng.tensor_scalar(out=mx, in0=mx, scalar1=-SCALE, scalar2=None,
                                  op0=ALU.mult)
            W[f"mask_{w}_{h}_{Qc}"] = minc("dve", i)

        def soft_act(eng, w, h, Qc):
            nb, nw, db, doff = SC_LAYOUT[Qc]
            eng.wait_ge(sems["dve"], W[f"mask_{w}_{h}_{Qc}"])
            P = P_sb[Qc % 2]
            mx = stat[:, 40:41]
            a1 = stat[:, 44:45]; a2 = stat[:, 45:46]
            eng.activation(out=P[:, 0:nw], in_=pbank[nb][:, 0:nw], func=AF.Exp,
                           scale=SCALE, bias=mx, accum_out=a1)
            i = eng.activation(out=P[:, nw:nw + 128], in_=sd_sb[:, :], func=AF.Exp,
                               scale=SCALE, bias=mx, accum_out=a2)
            W[f"exp_{w}_{h}_{Qc}"] = minc("act", i)
            set_guard(nb, "act", ms["act"])
            if db != nb:
                set_guard(db, "act", ms["act"])

        def soft_norm_dve(eng, w, h, Qc):
            eng.wait_ge(sems["act"], W[f"exp_{w}_{h}_{Qc}"])
            P = P_sb[Qc % 2]
            a1 = stat[:, 44:45]; a2 = stat[:, 45:46]; rs = stat[:, 46:47]
            nw = 16 + (Qc + 1) * 128
            i = eng.tensor_add(out=rs, in0=a1, in1=a2)
            hop("dve", eng, i)
            i = eng.reciprocal(out=rs, in_=rs)
            hop("dve", eng, i)
            i = eng.tensor_scalar(out=P[:, 0:nw], in0=P[:, 0:nw], scalar1=rs,
                                  scalar2=None, op0=ALU.mult)
            W[f"pnorm_{w}_{h}_{Qc}"] = minc("dve", i)

        def pt_pe(eng, w, h, Qc):
            eng.wait_ge(sems["dve"], W[f"pnorm_{w}_{h}_{Qc}"])
            P = P_sb[Qc % 2]
            bk = Qc % 2
            wait_bank(eng, bk)
            eng.matmul(pbank[bk][0:16, 0:128], P[:, 0:16], I128[:, :],
                       start=True, stop=True)
            i = None
            for tj in range(Qc + 1):
                if tj < 3:
                    out = pbank[bk][:, 128 + tj * 128:128 + (tj + 1) * 128]
                else:
                    bk2 = 1 - bk
                    wait_bank(eng, bk2)
                    out = pbank[bk2][:, 0:128]
                i = eng.matmul(out, P[:, 16 + tj * 128:16 + (tj + 1) * 128],
                               I128[:, :], start=True, stop=True)
            W[f"ptmm_{w}_{h}_{Qc}"] = minc("pe", i)

        def pt_evac_act(eng, w, h, Qc):
            bk = Qc % 2
            eng.wait_ge(sems["pe"], W[f"ptmm_{w}_{h}_{Qc}"])
            i = eng.activation(out=pmPT[0:16, Qc * 128:(Qc + 1) * 128],
                               in_=pbank[bk][0:16, 0:128], func=AF.Copy)
            for tj in range(Qc + 1):
                if tj < 3:
                    sp = pbank[bk][:, 128 + tj * 128:128 + (tj + 1) * 128]
                else:
                    sp = pbank[1 - bk][:, 0:128]
                i = eng.activation(out=PT_t[:, tj, Qc * 128:(Qc + 1) * 128],
                                   in_=sp, func=AF.Copy)
            m = minc("act", i)
            W[f"ptev_{w}_{h}_{Qc}"] = m
            set_guard(bk, "act", m)
            if Qc == 3:
                set_guard(1 - bk, "act", m)

        def av_mm(eng, w, h, Qc):
            eng.wait_ge(sems["act"], W[f"ptev_{w}_{h}_{Qc}"])
            if Qc == 0:
                eng.wait_ge(sems["act"], W[f"vev_{w}_{h//2}"])
                if w == 0 and h == 0:
                    eng.wait_ge(sems["d_c2"], W["c2_dmas"])
            bank = 6 + (h % 2)
            if Qc == 0:
                wait_bank(eng, bank)
            out = pbank[bank][:, Qc * 128:(Qc + 1) * 128]
            eng.matmul(out, pmv_t[:, h * 128:(h + 1) * 128],
                       pmPT[:, Qc * 128:(Qc + 1) * 128], start=True, stop=False)
            i = None
            for tj in range(Qc + 1):
                i = eng.matmul(out, v_sb[tj][:, h * 128:(h + 1) * 128],
                               PT_t[:, tj, Qc * 128:(Qc + 1) * 128],
                               start=False, stop=(tj == Qc))
            W[f"av_{w}_{h}_{Qc}"] = minc("pe", i)

        def av_evac_act(eng, w, h):
            bank = 6 + (h % 2)
            eng.wait_ge(sems["pe"], W[f"av_{w}_{h}_3"])
            i = eng.activation(out=attn_sb[:, h, :], in_=pbank[bank][:, :], func=AF.Copy)
            W[f"attnev_{w}_{h}"] = minc("act", i)
            set_guard(bank, "act", ms["act"])

        def wo_load(eng, w, os_):
            par = os_ % 2
            if w == 0 and os_ == 0 and GHOST:
                eng.wait_ge(sems["cc"], 3)
            if os_ < 2:
                eng.wait_ge(sems["pe"], W[f"vuse_{w}_{6+os_}"])
            else:
                eng.wait_ge(sems["pe"], W[f"wouse_{w}_{os_-2}"])
            dinc(f"d_vw{par}", eng.dma_start(
                out=bigw[par][:, :, :],
                in_=woutT_v[:, :, os_ * 256:(os_ + 1) * 256]))
            W[f"wold_{w}_{os_}"] = dma[f"d_vw{par}"]

        def wo_mm(eng, w, os_):
            par = os_ % 2
            eng.wait_ge(sems[f"d_vw{par}"], W[f"wold_{w}_{os_}"])
            if os_ == 0:
                eng.wait_ge(sems["act"], W[f"attnev_{w}_15"])
            for qc in range(4):
                bank = 2 + (os_ * 4 + qc) % 4
                wait_bank(eng, bank)
                for hh in range(16):
                    i = eng.matmul(pbank[bank][:, 0:256],
                                   attn_sb[:, hh, qc * 128:(qc + 1) * 128],
                                   bigw[par][:, hh, :], start=(hh == 0), stop=(hh == 15))
                W[f"womm_{w}_{os_}_{qc}"] = minc("pe", i)
            W[f"wouse_{w}_{os_}"] = ms["pe"]

        def wo_resid_dve(eng, w, os_, qc):
            bank = 2 + (os_ * 4 + qc) % 4
            eng.wait_ge(sems["pe"], W[f"womm_{w}_{os_}_{qc}"])
            i = eng.tensor_add(
                out=x_sb[qc][:, os_ * 256:(os_ + 1) * 256],
                in0=pbank[bank][:, 0:256],
                in1=x_sb[qc][:, os_ * 256:(os_ + 1) * 256])
            W[f"worid_{w}_{os_}_{qc}"] = minc("dve", i)
            set_guard(bank, "dve", ms["dve"])

        def gu_slot(mat, ic):
            s = ic % 3
            base = bigw[mat] if s < 2 else qkw[mat]
            if s == 0:
                return base.ap()[:, :, 0:128]
            if s == 1:
                return base.ap()[:, :, 128:256]
            return base.ap()[:, :, :]

        def ffn_gu_load(eng, w, mat, ic):
            par = mat
            if ic == 0:
                eng.wait_ge(sems["cc"], mat + 1)
            if ic >= 3:
                eng.wait_ge(sems["pe"], W[f"guuse_{w}_{mat}_{ic-3}"])
            elif ic < 2:
                eng.wait_ge(sems["pe"], W[f"wouse_{w}_{6+mat}"])
            else:
                eng.wait_ge(sems["pe"], W[f"qkuse_{w}_{30 + mat}"])
            src = (WgT_v if mat == 0 else WuT_v)[:, :, ic * 128:(ic + 1) * 128]
            dinc(f"d_g{par}", eng.dma_start(out=gu_slot(mat, ic), in_=src))
            W[f"guld_{w}_{mat}_{ic}"] = dma[f"d_g{par}"]

        def ffn_gu_mm(eng, w, mat, ic):
            par = mat
            eng.wait_ge(sems[f"d_g{par}"], W[f"guld_{w}_{mat}_{ic}"])
            if mat == 0 and ic == 0:
                eng.wait_ge(sems["act"], W[f"xT_{w}_3"])
            slot = gu_slot(mat, ic)
            bank = (2 if mat == 0 else 3) + 2 * (ic % 2)
            wait_bank(eng, bank)
            i = None
            for dc in range(16):
                i = eng.matmul(pbank[bank][:, :], slot[:, dc, :],
                               xnT[:, dc, :], start=(dc == 0), stop=(dc == 15))
            W[f"gumm_{w}_{mat}_{ic}"] = minc("pe", i)
            W[f"guuse_{w}_{mat}_{ic}"] = ms["pe"]

        def ffn_silu_act(eng, w, ic):
            bank = 2 + 2 * (ic % 2)
            eng.wait_ge(sems["pe"], W[f"gumm_{w}_0_{ic}"])
            if ic > 0:
                eng.wait_ge(sems["dve"], W[f"h_{w}_{ic-1}"])
            i = eng.activation(out=silu_sb[:, :], in_=pbank[bank][:, :], func=AF.Silu)
            W[f"silu_{w}_{ic}"] = minc("act", i)
            set_guard(bank, "act", ms["act"])

        def ffn_h_dve(eng, w, ic):
            bank = 3 + 2 * (ic % 2)
            eng.wait_ge(sems["act"], W[f"silu_{w}_{ic}"])
            eng.wait_ge(sems["pe"], W[f"gumm_{w}_1_{ic}"])
            i = eng.tensor_mul(out=h_sb[:, ic, :], in0=pbank[bank][:, :],
                               in1=silu_sb[:, :])
            W[f"h_{w}_{ic}"] = minc("dve", i)
            set_guard(bank, "dve", ms["dve"])

        def ffn_dw_load(eng, w, oh, sl):
            idx = oh * 11 + sl
            par = sl % 2
            if w == 0 and idx == 0:
                eng.wait_ge(sems["cc"], 3)
            prev = W.get(f"dwprev_{w}_{par}")
            if prev is not None:
                eng.wait_ge(sems["pe"], prev)
            else:
                eng.wait_ge(sems["pe"], W[f"guuse_{w}_{par}_42"])
            nic = min(4, NIC - sl * 4)
            src = WdT_v[:, sl * 4: sl * 4 + nic, oh * 1024:(oh + 1) * 1024]
            dinc(f"d_g{par}", eng.dma_start(out=bigw_d[par][:, 0:nic, :], in_=src))
            W[f"dwld_{w}_{oh}_{sl}"] = dma[f"d_g{par}"]

        def ffn_dw_mm(eng, w, oh, sl):
            par = sl % 2
            eng.wait_ge(sems[f"d_g{par}"], W[f"dwld_{w}_{oh}_{sl}"])
            nic = min(4, NIC - sl * 4)
            i = None
            for il in range(nic):
                ic = sl * 4 + il
                if oh == 0:
                    eng.wait_ge(sems["dve"], W[f"h_{w}_{ic}"])
                if ic == 0:
                    for bk in range(8):
                        wait_bank(eng, bk)
                for tc in range(4):
                    for os_ in range(2):
                        bank = tc * 2 + os_
                        i = eng.matmul(pbank[bank][:, :],
                                       h_sb[:, ic, tc * 128:(tc + 1) * 128],
                                       bigw_d[par][:, il, os_ * 512:(os_ + 1) * 512],
                                       start=(ic == 0), stop=(ic == NIC - 1))
            W[f"dwuse_{w}_{oh}_{sl}"] = minc("pe", i)
            W[f"dwprev_{w}_{sl % 2}"] = ms["pe"]
            if sl == 10:
                W[f"dwdone_{w}_{oh}"] = ms["pe"]

        def ffn_dw_resid_dve(eng, w, oh):
            eng.wait_ge(sems["pe"], W[f"dwdone_{w}_{oh}"])
            for tc in range(4):
                for os_ in range(2):
                    bank = tc * 2 + os_
                    i = eng.tensor_add(
                        out=x_sb[tc][:, oh * 1024 + os_ * 512: oh * 1024 + (os_ + 1) * 512],
                        in0=pbank[bank][:, :],
                        in1=x_sb[tc][:, oh * 1024 + os_ * 512: oh * 1024 + (os_ + 1) * 512])
                    set_guard(bank, "dve", minc("dve", i))
                if oh == 1:
                    W[f"xdone_{w}_{tc}"] = ms["dve"]

        def x_store(eng, w, tc):
            eng.wait_ge(sems["dve"], W[f"xdone_{w}_{tc}"])
            dinc("d_out", eng.dma_start(
                out=d_y[w * TOK + tc * 128: w * TOK + (tc + 1) * 128, :],
                in_=x_sb[tc][:, :]))
            W[f"store_{w}_{tc}"] = dma["d_out"]

        def dbg_dump(items):
            pe_v, dve_v, act_v, pool_v = ms["pe"], ms["dve"], ms["act"], ms["pool"]
            def go(eng):
                eng.wait_ge(sems["pe"], pe_v)
                eng.wait_ge(sems["dve"], dve_v)
                eng.wait_ge(sems["act"], act_v)
                eng.wait_ge(sems["pool"], pool_v)
                for ap_, row in items:
                    rows = ap_.shape[0]
                    dinc("d_out", eng.dma_start(out=d_y[row:row + rows, 0:ap_.shape[-1]],
                                                in_=ap_))
            block.gpsimd(go)

        for w in range(NWIN):
            for tc in range(4):
                block.scalar(lambda e, w=w, tc=tc: x_load(e, w, tc))
            block.gpsimd(lambda e, w=w: rope_build_pool(e, w))
            block.vector(lambda e, w=w: rope_build_dve(e, w))
            block.scalar(lambda e, w=w: rope_build_act(e, w))

            def ln1_stats_all(eng, w=w):
                i = None
                for tc in range(4):
                    i = ln_stats_act(eng, w, tc, "ln1")
                W[f"ln1st_{w}"] = minc("act", i)
            block.scalar(ln1_stats_all)
            block.vector(lambda e, w=w: ln_var_dve(e, w, "ln1"))
            block.scalar(lambda e, w=w: ln_rstd_act(e, w, "ln1"))
            block.vector(lambda e, w=w: ln1_rms_dve(e, w))
            block.scalar(lambda e, w=w: ln1_rms_act(e, w))
            block.vector(lambda e, w=w: ln1_ab_dve(e, w))
            for tc in range(4):
                block.vector(lambda e, w=w, tc=tc: xstage_dve(e, w, tc, "xn"))
                for grp in range(4):
                    block.tensor(lambda e, w=w, tc=tc, grp=grp: xT_pe_grp(e, w, tc, grp))
                    block.scalar(lambda e, w=w, tc=tc, grp=grp: xT_evac_grp(e, w, tc, grp))
                if w == 0:
                    ghost_tick(2)
            if STAGE <= 1:
                ghost_tick(n_ghost)
                dbg_dump([(xnT.ap().rearrange("p a b -> p (a b)")[:, j * 2048:(j + 1) * 2048],
                           j * 128) for j in range(4)]
                         + [(x_sb[j].ap()[:, :], 512 + j * 128) for j in range(4)]
                         + [(stat.ap()[:, :], 1024)])
                continue

            for hp in range(8):
                for which in range(4):
                    block.gpsimd(lambda e, w=w, hp=hp, which=which:
                                 qk_wload(e, w, hp, which))
                    block.tensor(lambda e, w=w, hp=hp, which=which:
                                 qk_mm(e, w, hp, which))
                block.vector(lambda e, w=w, hp=hp: rope_pair_dve(e, w, hp))
                block.gpsimd(lambda e, w=w, vs=hp: v_wload(e, w, vs))
                for tc in range(4):
                    block.tensor(lambda e, w=w, vs=hp, tc=tc: v_mm_tc(e, w, vs, tc))
                    block.scalar(lambda e, w=w, vs=hp, tc=tc: v_evac_tc(e, w, vs, tc))
                if w == 0:
                    ghost_tick(3)
                if STAGE <= 2:
                    continue

                for h in (2 * hp, 2 * hp + 1):
                    def chain(Qc, w=w, h=h):
                        block.tensor(lambda e, w=w, h=h, Qc=Qc: sc_mm(e, w, h, Qc))
                        block.vector(lambda e, w=w, h=h, Qc=Qc: soft_dve(e, w, h, Qc))
                        block.scalar(lambda e, w=w, h=h, Qc=Qc: soft_act(e, w, h, Qc))
                        block.vector(lambda e, w=w, h=h, Qc=Qc:
                                     soft_norm_dve(e, w, h, Qc))
                    def ptav(Qc, w=w, h=h):
                        block.tensor(lambda e, w=w, h=h, Qc=Qc: pt_pe(e, w, h, Qc))
                        block.scalar(lambda e, w=w, h=h, Qc=Qc:
                                     pt_evac_act(e, w, h, Qc))
                        block.tensor(lambda e, w=w, h=h, Qc=Qc: av_mm(e, w, h, Qc))
                    chain(0)
                    chain(1)
                    ptav(0)
                    chain(2)
                    ptav(1)
                    chain(3)
                    ptav(2)
                    ptav(3)
                    block.scalar(lambda e, w=w, h=h: av_evac_act(e, w, h))
                    if w == 0:
                        ghost_tick(1)

            if w == 0 and GHOST:
                ghost_tick(n_ghost)
                for mat in range(3):
                    block.gpsimd(lambda e, mat=mat: ghost_collective(e, mat))
            if STAGE <= 3:
                dbg_dump([(attn_sb.ap().rearrange("p a b -> p (a b)")[:, j * 2048:(j + 1) * 2048],
                           j * 128) for j in range(4)]
                         + [(P_sb[1].ap()[:, 0:544], 512)]
                         + [(qh[0].ap()[:, :], 640), (kh[0].ap()[:, :], 768)])
                continue

            n_os = 8 if STAGE < 30 or STAGE >= 40 else (STAGE - 30)
            for os_ in range(n_os):
                block.gpsimd(lambda e, w=w, os_=os_: wo_load(e, w, os_))
                block.tensor(lambda e, w=w, os_=os_: wo_mm(e, w, os_))
                for qc in range(4):
                    block.vector(lambda e, w=w, os_=os_, qc=qc:
                                 wo_resid_dve(e, w, os_, qc))

            if STAGE <= 4 or (30 <= STAGE < 40):
                if STAGE == 4 or STAGE == 38:
                    dbg_dump([(x_sb[j].ap()[:, :], j * 128) for j in range(4)])
                continue

            def ln2_stats_all(eng, w=w):
                i = None
                for tc in range(4):
                    i = ln_stats_act(eng, w, tc, "ln2")
                W[f"ln2st_{w}"] = minc("act", i)
            block.scalar(ln2_stats_all)
            block.vector(lambda e, w=w: ln_var_dve(e, w, "ln2"))
            block.scalar(lambda e, w=w: ln_rstd_act(e, w, "ln2"))
            block.vector(lambda e, w=w: ln2_ab_dve(e, w))
            for tc in range(4):
                block.vector(lambda e, w=w, tc=tc: xstage_dve(e, w, tc, "x2"))
                for grp in range(4):
                    block.tensor(lambda e, w=w, tc=tc, grp=grp: xT_pe_grp(e, w, tc, grp))
                    block.scalar(lambda e, w=w, tc=tc, grp=grp: xT_evac_grp(e, w, tc, grp))

            if STAGE <= 5:
                dbg_dump([(xnT.ap().rearrange("p a b -> p (a b)")[:, j * 2048:(j + 1) * 2048],
                           j * 128) for j in range(4)]
                         + [(x_sb[j].ap()[:, :], 512 + j * 128) for j in range(4)]
                         + [(stat.ap()[:, :], 1024)])
                continue
            for ic in range(NIC):
                for mat in range(2):
                    block.sync(lambda e, w=w, mat=mat, ic=ic: ffn_gu_load(e, w, mat, ic))
                for mat in range(2):
                    block.tensor(lambda e, w=w, mat=mat, ic=ic: ffn_gu_mm(e, w, mat, ic))
                block.scalar(lambda e, w=w, ic=ic: ffn_silu_act(e, w, ic))
                block.vector(lambda e, w=w, ic=ic: ffn_h_dve(e, w, ic))

            if STAGE <= 6:
                dbg_dump([(h_sb.ap().rearrange("p a b -> p (a b)")[:, j * 2048:(j + 1) * 2048],
                           j * 128) for j in range(8)])
                continue
            for oh in range(2):
                for sl in range(11):
                    block.sync(lambda e, w=w, oh=oh, sl=sl: ffn_dw_load(e, w, oh, sl))
                    block.tensor(lambda e, w=w, oh=oh, sl=sl: ffn_dw_mm(e, w, oh, sl))
                block.vector(lambda e, w=w, oh=oh: ffn_dw_resid_dve(e, w, oh))
            W[f"bigw_free_{w+1}_0"] = W[f"dwprev_{w}_0"]
            W[f"bigw_free_{w+1}_1"] = W[f"dwprev_{w}_1"]
            for tc in range(4):
                block.sync(lambda e, w=w, tc=tc: x_store(e, w, tc))

        def final_wait(eng):
            if GHOST and ms["cc"] >= 3:
                eng.wait_ge(sems["cc"], 3)
            eng.wait_ge(sems["d_out"], dma["d_out"])
        block.sync(final_wait)

        real_block = ctx.enter_context(nc.Block())
        real_block.sync(block.recs["sync"].replay)
        real_block.vector(block.recs["vector"].replay)
        real_block.scalar(block.recs["scalar"].replay)
        real_block.tensor(block.recs["tensor"].replay)
        real_block.gpsimd(block.recs["gpsimd"].replay)

    return nc


def _host_prep(L0_attn_x, L0_attn_norm_w, L0_attn_w_qkv, L0_attn_w_out,
               L0_attn_pm, L0_ffn_lut, L0_ffn_gate_idx, L0_ffn_up_idx,
               L0_ffn_down_idx):
    perm_eo = np.concatenate([np.arange(0, 128, 2), np.arange(1, 128, 2)])
    order = []
    for part in range(2):
        base = part * 2048
        for hp in range(8):
            A, B = 2 * hp, 2 * hp + 1
            order += [base + A * 128 + j for j in perm_eo[:64]]
            order += [base + B * 128 + j for j in perm_eo[:64]]
            order += [base + A * 128 + j for j in perm_eo[64:]]
            order += [base + B * 128 + j for j in perm_eo[64:]]
    order += list(range(4096, 6144))
    wqkvT = np.ascontiguousarray(L0_attn_w_qkv[order, :].T)
    woutT = np.ascontiguousarray(L0_attn_w_out.T)
    pmkE = L0_attn_pm[0][:, :, 0::2].transpose(2, 0, 1).reshape(64, 256)
    pmkO = L0_attn_pm[0][:, :, 1::2].transpose(2, 0, 1).reshape(64, 256)
    half = np.concatenate([pmkE, pmkO], axis=1)
    pmk = np.ascontiguousarray(np.concatenate([half, half], axis=0))
    pmv = np.ascontiguousarray(L0_attn_pm[1].transpose(1, 0, 2).reshape(16, 2048))
    gidxT = np.ascontiguousarray(L0_ffn_gate_idx.T)
    uidxT = np.ascontiguousarray(L0_ffn_up_idx.T)
    didxT = np.ascontiguousarray(L0_ffn_down_idx.T)
    common = {
        "wqkvT": wqkvT, "woutT": woutT, "pmk": pmk, "pmv": pmv,
        "normw": np.ascontiguousarray(L0_attn_norm_w.reshape(1, D)),
        "lut": np.ascontiguousarray(L0_ffn_lut.reshape(1, 16)),
    }
    in_maps = []
    for c in range(NCORE):
        m = dict(common)
        m["x"] = np.ascontiguousarray(L0_attn_x[c])
        m["gidx"] = np.ascontiguousarray(gidxT[c * GSH:(c + 1) * GSH])
        m["uidx"] = np.ascontiguousarray(uidxT[c * GSH:(c + 1) * GSH])
        m["didx"] = np.ascontiguousarray(didxT[c * DSH:(c + 1) * DSH])
        in_maps.append(m)
    return in_maps


def kernel(L0_attn_x, L0_attn_norm_w, L0_attn_w_qkv, L0_attn_w_out, L0_attn_pm,
           L0_ffn_lut, L0_ffn_gate_idx, L0_ffn_up_idx, L0_ffn_down_idx):
    in_maps = _host_prep(np.asarray(L0_attn_x), np.asarray(L0_attn_norm_w),
                         np.asarray(L0_attn_w_qkv), np.asarray(L0_attn_w_out),
                         np.asarray(L0_attn_pm), np.asarray(L0_ffn_lut),
                         np.asarray(L0_ffn_gate_idx), np.asarray(L0_ffn_up_idx),
                         np.asarray(L0_ffn_down_idx))
    nc = build_program()
    res = run_bass_kernel_spmd(nc, in_maps, core_ids=list(range(NCORE)))
    return np.stack([res.results[c]["y"] for c in range(NCORE)], axis=0)


# version 06
# speedup vs baseline: 1.0353x; 1.0353x over previous
import numpy as np
from contextlib import ExitStack

import concourse.bass as bass
import concourse.mybir as mybir
from concourse.bass_utils import run_bass_kernel_spmd

F32 = mybir.dt.float32
BF16 = mybir.dt.bfloat16
I32 = mybir.dt.int32
AF = mybir.ActivationFunctionType
ALU = mybir.AluOpType
AX = mybir.AxisListType

D = 2048
H = 16
TOK = 512
NWIN = 4
INTER = 5504
NIC = 43
NEG = -3.4028235e38
SCALE = float(128.0) ** -0.5
LN2PI_INV = float(1.0 / (2.0 * np.pi))
CW1 = 6.28125
CW2 = float(np.float32(2.0 * np.pi - 6.28125))
CW3 = float(2.0 * np.pi - CW1 - float(np.float32(CW2)))
LN1E4_64 = float(np.log(10000.0) / 64.0)
NCORE = 8
STAGE = 99


class _Stub:
    __slots__ = ("incs",)

    def __init__(self):
        self.incs = []

    def then_inc(self, sem, n=1):
        self.incs.append((sem, n))
        return self


class _Rec:

    def __init__(self):
        self.log = []

    def __getattr__(self, name):
        def call(*a, **k):
            stub = _Stub()
            self.log.append((name, a, k, stub))
            return stub
        return call

    def replay(self, eng):
        for name, a, k, stub in self.log:
            instr = getattr(eng, name)(*a, **k)
            for sem, n in stub.incs:
                instr.then_inc(sem, n)


class _EmitShim:
    def __init__(self):
        self.recs = {n: _Rec() for n in ("sync", "vector", "scalar", "tensor",
                                         "gpsimd")}

    def _mk(self, n):
        return lambda f: f(self.recs[n])

    def __getattr__(self, n):
        if n in ("sync", "vector", "scalar", "tensor", "gpsimd"):
            return self._mk(n)
        raise AttributeError(n)
DMASCRATCH = 16384
GHOST = 1
GSH = D // NCORE
DSH = INTER // NCORE


def build_program():
    nc = bass.Bass("TRN2", target_bir_lowering=False,
                   dynamic_dma_scratch_size=DMASCRATCH,
                   detect_race_conditions=False)

    d_x = nc.dram_tensor("x", [2048, D], F32, kind="ExternalInput").ap()
    d_wqkvT = nc.dram_tensor("wqkvT", [D, 3 * D], F32, kind="ExternalInput").ap()
    d_woutT = nc.dram_tensor("woutT", [D, D], F32, kind="ExternalInput").ap()
    d_normw = nc.dram_tensor("normw", [1, D], F32, kind="ExternalInput").ap()
    d_pmk = nc.dram_tensor("pmk", [128, 512], F32, kind="ExternalInput").ap()
    d_pmv = nc.dram_tensor("pmv", [16, 2048], F32, kind="ExternalInput").ap()
    d_lut = nc.dram_tensor("lut", [1, 16], F32, kind="ExternalInput").ap()
    d_gidx = nc.dram_tensor("gidx", [GSH, INTER], I32, kind="ExternalInput").ap()
    d_uidx = nc.dram_tensor("uidx", [GSH, INTER], I32, kind="ExternalInput").ap()
    d_didx = nc.dram_tensor("didx", [DSH, D], I32, kind="ExternalInput").ap()
    d_y = nc.dram_tensor("y", [2048, D], F32, kind="ExternalOutput").ap()

    cc_in_g = nc.dram_tensor("cc_in_g", [GSH, INTER], BF16, kind="Internal").ap()
    cc_in_u = nc.dram_tensor("cc_in_u", [GSH, INTER], BF16, kind="Internal").ap()
    cc_in_d = nc.dram_tensor("cc_in_d", [DSH, D], BF16, kind="Internal").ap()
    WgT_d = nc.dram_tensor("WgT_d", [D, INTER], BF16, kind="Internal").ap()
    WuT_d = nc.dram_tensor("WuT_d", [D, INTER], BF16, kind="Internal").ap()
    WdT_d = nc.dram_tensor("WdT_d", [INTER, D], BF16, kind="Internal").ap()
    WgT_v = WgT_d.rearrange("(dc p) i -> p dc i", p=128)
    WuT_v = WuT_d.rearrange("(dc p) i -> p dc i", p=128)
    WdT_v = WdT_d.rearrange("(ic p) o -> p ic o", p=128)
    wqkvT_v = d_wqkvT.rearrange("(dc p) o -> p dc o", p=128)
    woutT_v = d_woutT.rearrange("(dc p) o -> p dc o", p=128)

    ctx = ExitStack()
    with ctx:
        sbt = lambda name, shape, dt: ctx.enter_context(nc.sbuf_tensor(name, shape, dt))
        pst = lambda name, shape: ctx.enter_context(nc.psum_tensor(name, shape, F32))

        x_sb = [sbt(f"x{i}", [128, D], F32) for i in range(4)]
        normw_b = sbt("normw_b", [128, D], BF16)
        lutb = sbt("lutb", [128, 16], F32)
        Btri = sbt("Btri", [128, 128], F32)
        B0 = sbt("B0", [128, 128], F32)
        I128 = sbt("I128", [128, 128], BF16)
        pmkP = sbt("pmkP", [128, 512], BF16)
        pmv_t = sbt("pmv_t", [16, 2048], BF16)
        pio = sbt("pio", [128, 1], F32)
        invf2 = sbt("invf2", [128, 1], F32)
        kvec = sbt("kvec", [128, 8], F32)
        ones_row = sbt("ones_row", [1, 128], F32)
        lut_row = sbt("lut_row", [1, 16], F32)
        normw_row = sbt("normw_row", [1, D], F32)
        scratch8 = sbt("scratch8", [128, 4 * TOK], F32)
        Ctab = scratch8.ap()[:, 0:TOK]
        Stab = scratch8.ap()[:, TOK:2 * TOK]
        ropet1 = scratch8.ap()[:, 2 * TOK:3 * TOK]
        ropet2 = scratch8.ap()[:, 3 * TOK:4 * TOK]
        upslot1 = scratch8.ap().bitcast(BF16).rearrange(
            "p (dc i) -> p dc i", dc=16)
        stat = sbt("stat", [128, 48], F32)
        xn_sb = sbt("xn_sb", [128, D], BF16)
        xn2_sb = sbt("xn2_sb", [128, D], BF16)
        xnT = sbt("xnT", [128, 16, TOK], BF16)
        qh = [sbt(f"qh{i}", [128, TOK], BF16) for i in range(2)]
        kh = [sbt(f"kh{i}", [128, TOK], BF16) for i in range(2)]
        v_sb = [sbt(f"v{i}", [128, D], BF16) for i in range(4)]
        attn_sb = sbt("attn_sb", [128, 16, TOK], BF16)
        P_sb = [sbt(f"P{i}", [128, 544], BF16) for i in range(2)]
        pmPT = sbt("pmPT", [16, TOK], BF16)
        PT_t = sbt("PT_t", [128, 4, TOK], BF16)
        sd_sb = sbt("sd_sb", [128, 128], F32)
        silu_sb = sbt("silu_sb", [128, TOK], F32)
        rope_f = silu_sb
        h_sb = sbt("h_sb", [128, NIC, TOK], BF16) if (STAGE > 5 and not 30 <= STAGE < 40) else None
        qkw = sbt("qkw", [128, 16, 256], BF16)
        bigw = [sbt(f"bigw{i}", [128, 16, 256], BF16) for i in range(2)]
        gidx_b = [sbt(f"gidx{i}", [128, 688], I32) for i in range(2)]
        gbf = sbt("gbf", [128, 688], BF16)
        gmask = sbt("gmask", [128, 688], BF16)
        gacc = [sbt(f"gacc{i}", [128, 688], BF16) for i in range(2)]

        bigw_d = [b.ap().rearrange("p a b -> p (a b)").rearrange(
            "p (ic o) -> p ic o", o=1024) for b in bigw]

        pbank = [pst(f"pb{i}", [128, 512]) for i in range(8)]

        sems = {}
        for s_ in ["pe", "dve", "act", "pool", "cc",
                   "d_x", "d_out", "d_qk0", "d_qk1", "d_vw0", "d_vw1",
                   "d_g0", "d_g1", "d_idx", "d_sh", "d_c", "d_c2"]:
            sems[s_] = ctx.enter_context(nc.semaphore(s_))
        block = _EmitShim()

        ms = {"pe": 0, "dve": 0, "act": 0, "pool": 0, "cc": 0}
        dma = {k: 0 for k in ["d_x", "d_out", "d_qk0", "d_qk1", "d_vw0",
                              "d_vw1", "d_g0", "d_g1", "d_idx", "d_sh", "d_c",
                              "d_c2"]}
        bank_guard = [None] * 8
        W = {}

        def minc(key, instr):
            ms[key] += 1
            instr.then_inc(sems[key], 1)
            return ms[key]

        def dinc(key, instr):
            dma[key] += 16
            instr.then_inc(sems[key], 16)
            return dma[key]

        def wait_bank(eng, i):
            g = bank_guard[i]
            if g is not None:
                eng.wait_ge(sems[g[0]], g[1])
                bank_guard[i] = None

        def set_guard(i, key, val):
            bank_guard[i] = (key, val)

        def hop(key, eng, instr):
            v = minc(key, instr)
            eng.wait_ge(sems[key], v)

        def c_load(eng):
            dinc("d_c", eng.dma_start(out=normw_row[:, :], in_=d_normw))
            dinc("d_c", eng.dma_start(out=lut_row[:, :], in_=d_lut))
        block.sync(c_load)

        def c_load2(eng):
            dinc("d_c2", eng.dma_start(out=pmkP[:, :], in_=d_pmk))
            dinc("d_c2", eng.dma_start(out=pmv_t[:, :], in_=d_pmv))
        block.gpsimd(c_load2)
        W["c_dmas"] = dma["d_c"]
        W["c2_dmas"] = dma["d_c2"]

        def c_pool(eng):
            eng.iota(pio[:, 0:1], pattern=[[0, 1]], base=0, channel_multiplier=1,
                     allow_small_or_imprecise_dtypes=True)
            eng.iota(Btri[:, :], pattern=[[1, 128]], base=0, channel_multiplier=0,
                     allow_small_or_imprecise_dtypes=True)
            eng.memset(kvec[:, 0:1], 1e-5)
            eng.memset(kvec[:, 1:2], 1e-6)
            eng.memset(kvec[:, 2:3], float(np.pi / 2))
            i = eng.memset(ones_row[:, :], 1.0)
            minc("pool", i)
        block.gpsimd(c_pool)

        def c_dve(eng):
            eng.wait_ge(sems["pool"], 1)
            eng.tensor_scalar(out=I128[:, :], in0=Btri[:, :], scalar1=pio[:, 0:1],
                              scalar2=None, op0=ALU.is_equal)
            i = eng.tensor_scalar(out=B0[:, :], in0=Btri[:, :], scalar1=pio[:, 0:1],
                                  scalar2=NEG, op0=ALU.is_gt, op1=ALU.mult)
            hop("dve", eng, i)
            i = eng.tensor_copy(out=Btri[:, :], in_=B0[:, :])
            hop("dve", eng, i)
            eng.memset(B0[:, 0:16], 0.0)
            i = eng.tensor_scalar(out=invf2[:, 0:1], in0=pio[:, 0:1], scalar1=64.0,
                                  scalar2=64.0, op0=ALU.is_ge, op1=ALU.mult)
            hop("dve", eng, i)
            i = eng.tensor_sub(out=invf2[:, 0:1], in0=pio[:, 0:1], in1=invf2[:, 0:1])
            minc("dve", i)
        block.vector(c_dve)

        def c_act(eng):
            eng.wait_ge(sems["dve"], ms["dve"])
            i = eng.activation(out=invf2[:, 0:1], in_=invf2[:, 0:1], func=AF.Exp,
                               scale=-LN1E4_64)
            minc("act", i)
        block.scalar(c_act)

        def c_pe(eng):
            eng.wait_ge(sems["d_c"], W["c_dmas"])
            eng.wait_ge(sems["pool"], 1)
            eng.matmul(pbank[0][:, 0:16], ones_row[:, :], lut_row[:, :],
                       start=True, stop=True)
            for j in range(4):
                i = eng.matmul(pbank[1 + j][:, :], ones_row[:, :],
                               normw_row[:, j * 512:(j + 1) * 512],
                               start=True, stop=True)
            minc("pe", i)
        block.tensor(c_pe)
        W["pe_bcast"] = ms["pe"]

        def c_dve2(eng):
            eng.wait_ge(sems["pe"], W["pe_bcast"])
            eng.tensor_copy(out=lutb[:, :], in_=pbank[0][:, 0:16])
            for j in range(4):
                i = eng.tensor_copy(out=normw_b[:, j * 512:(j + 1) * 512],
                                    in_=pbank[1 + j][:, :])
            minc("dve", i)
        block.vector(c_dve2)
        W["consts_done"] = ms["dve"]
        for i in range(5):
            set_guard(i, "dve", W["consts_done"])

        ghost_tasks = []
        for (src, dst) in ((d_gidx, cc_in_g), (d_uidx, cc_in_u)):
            for rb in range(2):
                for cb in range(8):
                    ghost_tasks.append(
                        (src[rb * 128:(rb + 1) * 128, cb * 688:(cb + 1) * 688],
                         dst[rb * 128:(rb + 1) * 128, cb * 688:(cb + 1) * 688],
                         128, 688))
        for rb in range(6):
            rows = 128 if rb < 5 else 48
            for cb in range(4):
                ghost_tasks.append(
                    (d_didx[rb * 128:rb * 128 + rows, cb * 512:(cb + 1) * 512],
                     cc_in_d[rb * 128:rb * 128 + rows, cb * 512:(cb + 1) * 512],
                     rows, 512))
        n_ghost = len(ghost_tasks)
        gstate = {"next": 0}

        def ghost_load(eng, k):
            src, dst, rows, width = ghost_tasks[k]
            if k >= 2:
                eng.wait_ge(sems["dve"], W[f"ghost_dve_{k-2}"])
            dinc("d_idx", eng.dma_start(out=gidx_b[k % 2][0:rows, 0:width], in_=src))
            W[f"ghost_ld_{k}"] = dma["d_idx"]

        def ghost_dve(eng, k):
            src, dst, rows, width = ghost_tasks[k]
            eng.wait_ge(sems["d_idx"], W[f"ghost_ld_{k}"])
            if k >= 2:
                eng.wait_ge(sems["d_sh"], W[f"ghost_st_{k-2}"])
            g = gacc[k % 2]
            eng.tensor_copy(out=gbf[0:rows, 0:width], in_=gidx_b[k % 2][0:rows, 0:width])
            eng.tensor_scalar(out=g[0:rows, 0:width], in0=gbf[0:rows, 0:width],
                              scalar1=0.0, scalar2=lutb[0:rows, 0:1],
                              op0=ALU.is_equal, op1=ALU.mult)
            i = None
            for j in range(1, 16):
                eng.tensor_scalar(out=gmask[0:rows, 0:width], in0=gbf[0:rows, 0:width],
                                  scalar1=float(j), scalar2=lutb[0:rows, j:j + 1],
                                  op0=ALU.is_equal, op1=ALU.mult)
                i = eng.tensor_add(out=g[0:rows, 0:width], in0=g[0:rows, 0:width],
                                   in1=gmask[0:rows, 0:width])
            W[f"ghost_dve_{k}"] = minc("dve", i)

        def ghost_store(eng, k):
            src, dst, rows, width = ghost_tasks[k]
            eng.wait_ge(sems["dve"], W[f"ghost_dve_{k}"])
            dinc("d_sh", eng.dma_start(out=dst, in_=gacc[k % 2][0:rows, 0:width]))
            W[f"ghost_st_{k}"] = dma["d_sh"]
            if k + 1 == 16:
                W["cc_rdy_0"] = dma["d_sh"]
            if k + 1 == 32:
                W["cc_rdy_1"] = dma["d_sh"]
            if k + 1 == 56:
                W["cc_rdy_2"] = dma["d_sh"]

        def ghost_tick(n):
            if not GHOST:
                return
            lim = min(gstate["next"] + n, n_ghost)
            while gstate["next"] < lim:
                k = gstate["next"]
                block.sync(lambda e, k=k: ghost_load(e, k))
                block.vector(lambda e, k=k: ghost_dve(e, k))
                block.sync(lambda e, k=k: ghost_store(e, k))
                gstate["next"] = k + 1

        def ghost_collective(eng, mat):
            eng.wait_ge(sems["d_sh"], W[f"cc_rdy_{mat}"])
            io = [(cc_in_g, WgT_d), (cc_in_u, WuT_d), (cc_in_d, WdT_d)][mat]
            i = eng.collective_compute(
                "AllGather", ALU.bypass,
                replica_groups=[list(range(NCORE))],
                ins=[io[0].opt()], outs=[io[1].opt()])
            ms["cc"] += 1
            i.then_inc(sems["cc"], 1)
            eng.wait_ge(sems["cc"], ms["cc"])

        def x_load(eng, w, tc):
            if w > 0 and f"store_{w-1}_{tc}" in W:
                eng.wait_ge(sems["d_out"], W[f"store_{w-1}_{tc}"])
            dinc("d_x", eng.dma_start(
                out=x_sb[tc][:, :],
                in_=d_x[w * TOK + tc * 128: w * TOK + (tc + 1) * 128, :]))
            W[f"xload_{w}_{tc}"] = dma["d_x"]

        def rope_build_pool(eng, w):
            if w > 0 and STAGE > 5:
                eng.wait_ge(sems["dve"], W[f"h_{w-1}_{NIC-1}"])
            i = eng.iota(rope_f[:, :], pattern=[[1, TOK]], base=w * TOK,
                         channel_multiplier=0, allow_small_or_imprecise_dtypes=True)
            W[f"riota_{w}"] = minc("pool", i)

        def rope_build_dve(eng, w):
            eng.wait_ge(sems["pool"], W[f"riota_{w}"])
            eng.wait_ge(sems["act"], 1)
            eng.tensor_scalar(out=rope_f[:, :], in0=rope_f[:, :],
                              scalar1=invf2[:, 0:1], scalar2=None, op0=ALU.mult)
            eng.tensor_scalar(out=ropet1, in0=rope_f[:, :],
                              scalar1=LN2PI_INV, scalar2=None, op0=ALU.mult)
            rope_i = ropet2.bitcast(I32)
            eng.tensor_copy(out=rope_i, in_=ropet1)
            eng.tensor_copy(out=ropet1, in_=rope_i)
            eng.scalar_tensor_tensor(out=rope_f[:, :], in0=ropet1, scalar=-CW1,
                                     in1=rope_f[:, :], op0=ALU.mult, op1=ALU.add)
            eng.scalar_tensor_tensor(out=rope_f[:, :], in0=ropet1, scalar=-CW2,
                                     in1=rope_f[:, :], op0=ALU.mult, op1=ALU.add)
            eng.scalar_tensor_tensor(out=rope_f[:, :], in0=ropet1, scalar=-CW3,
                                      in1=rope_f[:, :], op0=ALU.mult, op1=ALU.add)
            PI = float(np.pi); TWO_PI = float(2.0 * np.pi)
            eng.tensor_scalar(out=ropet1, in0=rope_f[:, :], scalar1=PI,
                              scalar2=TWO_PI, op0=ALU.is_gt, op1=ALU.mult)
            eng.tensor_sub(out=rope_f[:, :], in0=rope_f[:, :], in1=ropet1)
            eng.tensor_scalar(out=ropet1, in0=rope_f[:, :], scalar1=-PI,
                              scalar2=TWO_PI, op0=ALU.is_lt, op1=ALU.mult)
            i = eng.tensor_add(out=rope_f[:, :], in0=rope_f[:, :], in1=ropet1)
            W[f"rred_{w}"] = minc("dve", i)

        def rope_build_act(eng, w):
            eng.wait_ge(sems["dve"], W[f"rred_{w}"])
            eng.activation(out=Stab, in_=rope_f[:, :], func=AF.Sin)
            eng.activation(out=ropet1, in_=rope_f[:, :], func=AF.Abs)
            i = eng.activation(out=Ctab, in_=ropet1, func=AF.Sin,
                               scale=-1.0, bias=kvec[:, 2:3])
            W[f"rtab_{w}"] = minc("act", i)

        def ln_stats_act(eng, w, tc, tag):
            if tag == "ln1":
                eng.wait_ge(sems["d_x"], W[f"xload_{w}_{tc}"])
            else:
                eng.wait_ge(sems["dve"], W[f"worid_{w}_7_{tc}"])
            eng.activation(out=xn_sb[:, :], in_=x_sb[tc][:, :], func=AF.Square,
                           accum_out=stat[:, 4 + tc:5 + tc])
            i = eng.activation(out=xn_sb[:, :], in_=x_sb[tc][:, :], func=AF.Copy,
                               accum_out=stat[:, tc:1 + tc])
            return i

        def ln_var_dve(eng, w, tag):
            eng.wait_ge(sems["act"], W[f"{tag}st_{w}"])
            sm = stat[:, 0:4]; sq = stat[:, 4:8]
            m = stat[:, 16:20]; var = stat[:, 20:24]; mm = stat[:, 24:28]
            i = eng.tensor_scalar(out=m, in0=sm, scalar1=1.0 / D, scalar2=None,
                                  op0=ALU.mult)
            hop("dve", eng, i)
            eng.tensor_mul(out=mm, in0=m, in1=m)
            i = eng.tensor_scalar(out=var, in0=sq, scalar1=1.0 / D, scalar2=None,
                                  op0=ALU.mult)
            hop("dve", eng, i)
            i = eng.tensor_sub(out=var, in0=var, in1=mm)
            W[f"{tag}var_{w}"] = minc("dve", i)

        def ln_rstd_act(eng, w, tag):
            eng.wait_ge(sems["dve"], W[f"{tag}var_{w}"])
            i = eng.activation(out=stat[:, 20:24], in_=stat[:, 20:24], func=AF.Ln,
                               bias=kvec[:, 0:1])
            hop("act", eng, i)
            i = eng.activation(out=stat[:, 20:24], in_=stat[:, 20:24], func=AF.Exp,
                               scale=-0.5)
            W[f"{tag}rstd_{w}"] = minc("act", i)

        def ln1_rms_dve(eng, w):
            eng.wait_ge(sems["act"], W[f"ln1rstd_{w}"])
            sm = stat[:, 0:4]; sq = stat[:, 4:8]; m = stat[:, 16:20]
            rstd = stat[:, 20:24]; t = stat[:, 24:28]; sy = stat[:, 28:32]
            i = eng.tensor_mul(out=t, in0=sm, in1=m)
            hop("dve", eng, i)
            eng.tensor_sub(out=t, in0=sq, in1=t)
            i = eng.tensor_mul(out=sy, in0=rstd, in1=rstd)
            hop("dve", eng, i)
            i = eng.tensor_mul(out=sy, in0=sy, in1=t)
            W[f"ln1sy_{w}"] = minc("dve", i)

        def ln1_rms_act(eng, w):
            eng.wait_ge(sems["dve"], W[f"ln1sy_{w}"])
            i = eng.activation(out=stat[:, 28:32], in_=stat[:, 28:32], func=AF.Ln,
                               scale=1.0 / D, bias=kvec[:, 1:2])
            hop("act", eng, i)
            i = eng.activation(out=stat[:, 28:32], in_=stat[:, 28:32], func=AF.Exp,
                               scale=-0.5)
            W[f"ln1rms_{w}"] = minc("act", i)

        def ln1_ab_dve(eng, w):
            eng.wait_ge(sems["act"], W[f"ln1rms_{w}"])
            a = stat[:, 8:12]; b = stat[:, 12:16]
            i = eng.tensor_mul(out=a, in0=stat[:, 20:24], in1=stat[:, 28:32])
            hop("dve", eng, i)
            i = eng.tensor_mul(out=b, in0=stat[:, 16:20], in1=a)
            hop("dve", eng, i)
            i = eng.tensor_scalar(out=b, in0=b, scalar1=-1.0, scalar2=None, op0=ALU.mult)
            hop("dve", eng, i)
            W[f"ln1ab_{w}"] = ms["dve"]

        def ln2_ab_dve(eng, w):
            eng.wait_ge(sems["act"], W[f"ln2rstd_{w}"])
            a = stat[:, 8:12]; b = stat[:, 12:16]
            i = eng.tensor_copy(out=a, in_=stat[:, 20:24])
            hop("dve", eng, i)
            i = eng.tensor_mul(out=b, in0=stat[:, 16:20], in1=a)
            hop("dve", eng, i)
            i = eng.tensor_scalar(out=b, in0=b, scalar1=-1.0, scalar2=None, op0=ALU.mult)
            hop("dve", eng, i)
            W[f"ln2ab_{w}"] = ms["dve"]

        def xstage_dve(eng, w, tc, tag):
            if tc > 0:
                eng.wait_ge(sems["pe"], W[f"xTmm_{w}_{tc-1}_3"])
            a = stat[:, 8 + tc:9 + tc]
            b = stat[:, 12 + tc:13 + tc]
            if tag == "xn":
                eng.tensor_scalar(out=xn2_sb[:, :], in0=x_sb[tc][:, :], scalar1=a,
                                  scalar2=b, op0=ALU.mult, op1=ALU.add)
                i = eng.tensor_mul(out=xn_sb[:, :], in0=xn2_sb[:, :], in1=normw_b[:, :])
            else:
                i = eng.tensor_scalar(out=xn_sb[:, :], in0=x_sb[tc][:, :], scalar1=a,
                                      scalar2=b, op0=ALU.mult, op1=ALU.add)
            W[f"stage_{w}_{tc}"] = minc("dve", i)

        def xT_pe_grp(eng, w, tc, grp):
            if grp == 0:
                eng.wait_ge(sems["dve"], W[f"stage_{w}_{tc}"])
            bk = grp % 2
            wait_bank(eng, bk)
            i = None
            for j in range(4):
                dc = grp * 4 + j
                i = eng.matmul(pbank[bk][:, j * 128:(j + 1) * 128],
                               xn_sb[:, dc * 128:(dc + 1) * 128], I128[:, :],
                               start=True, stop=True)
            W[f"xTmm_{w}_{tc}_{grp}"] = minc("pe", i)

        def xT_evac_grp(eng, w, tc, grp):
            bk = grp % 2
            eng.wait_ge(sems["pe"], W[f"xTmm_{w}_{tc}_{grp}"])
            dst = xnT.ap()[:, grp * 4:(grp + 1) * 4, tc * 128:(tc + 1) * 128]
            src = pbank[bk].ap().rearrange("p (j f) -> p j f", j=4)
            i = eng.activation(out=dst, in_=src, func=AF.Copy)
            minc("act", i)
            set_guard(bk, "act", ms["act"])
            W[f"xT_{w}_{tc}"] = ms["act"]

        def qk_wload(eng, w, hp, which):
            idx = hp * 4 + which
            par = idx % 2
            tile = (hp * 2 + (which & 1)) if which < 2 else (16 + hp * 2 + (which & 1))
            if idx >= 2:
                eng.wait_ge(sems["pe"], W[f"qkuse_{w}_{idx-2}"])
            elif w > 0:
                eng.wait_ge(sems["pe"], W[f"guuse_{w-1}_1_21"])
            dinc(f"d_qk{par}", eng.dma_start(
                out=qkw.ap()[:, :, par * 128:(par + 1) * 128],
                in_=wqkvT_v[:, :, tile * 128:(tile + 1) * 128]))
            W[f"qkload_{w}_{idx}"] = dma[f"d_qk{par}"]

        def qk_mm(eng, w, hp, which):
            idx = hp * 4 + which
            par = idx % 2
            bank = 2 + which
            eng.wait_ge(sems[f"d_qk{par}"], W[f"qkload_{w}_{idx}"])
            if hp == 0 and which == 0:
                eng.wait_ge(sems["act"], W[f"xT_{w}_3"])
            wait_bank(eng, bank)
            for dc in range(16):
                i = eng.matmul(pbank[bank][:, :],
                               qkw.ap()[:, dc, par * 128:(par + 1) * 128],
                               xnT[:, dc, :], start=(dc == 0), stop=(dc == 15))
            W[f"qkuse_{w}_{idx}"] = minc("pe", i)

        def rope_pair_dve(eng, w, hp):
            eng.wait_ge(sems["pe"], W[f"qkuse_{w}_{hp*4+3}"])
            eng.wait_ge(sems["act"], W[f"rtab_{w}"])
            qE, qO = qh[0], qh[1]
            kE, kO = kh[0], kh[1]
            i = None
            for (E, O, bE, bO) in ((qE, qO, 2, 3), (kE, kO, 4, 5)):
                eng.tensor_mul(out=ropet1, in0=pbank[bE][:, :], in1=Ctab)
                eng.tensor_mul(out=ropet2, in0=pbank[bO][:, :], in1=Stab)
                eng.tensor_sub(out=E[:, :], in0=ropet1, in1=ropet2)
                eng.tensor_mul(out=ropet1, in0=pbank[bO][:, :], in1=Ctab)
                eng.tensor_mul(out=ropet2, in0=pbank[bE][:, :], in1=Stab)
                i = eng.tensor_add(out=O[:, :], in0=ropet1, in1=ropet2)
            W[f"rope_{w}_{hp}"] = minc("dve", i)
            for b in (2, 3, 4, 5):
                set_guard(b, "dve", ms["dve"])

        def v_wload(eng, w, vs):
            par = vs % 2
            if vs >= 2:
                eng.wait_ge(sems["pe"], W[f"vuse_{w}_{vs-2}"])
            elif w > 0:
                eng.wait_ge(sems["pe"], W[f"bigw_free_{w}_{par}"])
            dinc(f"d_vw{par}", eng.dma_start(
                out=bigw[par][:, :, :],
                in_=wqkvT_v[:, :, 4096 + vs * 256: 4096 + (vs + 1) * 256]))
            W[f"vload_{w}_{vs}"] = dma[f"d_vw{par}"]

        def v_mm_tc(eng, w, vs, tc):
            par = vs % 2
            if tc == 0:
                eng.wait_ge(sems[f"d_vw{par}"], W[f"vload_{w}_{vs}"])
            bank = 6 + tc % 2
            wait_bank(eng, bank)
            i = None
            for dc in range(16):
                i = eng.matmul(pbank[bank][:, 0:256],
                               xnT[:, dc, tc * 128:(tc + 1) * 128],
                               bigw[par][:, dc, :], start=(dc == 0), stop=(dc == 15))
            W[f"vmm_{w}_{vs}_{tc}"] = minc("pe", i)
            if tc == 3:
                W[f"vuse_{w}_{vs}"] = ms["pe"]

        def v_evac_tc(eng, w, vs, tc):
            bank = 6 + tc % 2
            eng.wait_ge(sems["pe"], W[f"vmm_{w}_{vs}_{tc}"])
            i = eng.activation(out=v_sb[tc][:, vs * 256:(vs + 1) * 256],
                               in_=pbank[bank][:, 0:256], func=AF.Copy)
            minc("act", i)
            set_guard(bank, "act", ms["act"])
            W[f"vev_{w}_{vs}"] = ms["act"]

        SC_LAYOUT = {0: (2, 16, 2, 16), 1: (3, 144, 3, 144),
                     2: (4, 272, 4, 272), 3: (5, 400, 2, 0)}

        def sc_mm(eng, w, h, Qc):
            hp, half = h // 2, h % 2
            lo = half * 64
            nb, nw, db, doff = SC_LAYOUT[Qc]
            if Qc == 0 and half == 0:
                eng.wait_ge(sems["dve"], W[f"rope_{w}_{hp}"])
            if w == 0 and h == 0 and Qc == 0:
                eng.wait_ge(sems["d_c2"], W["c2_dmas"])
            wait_bank(eng, nb)
            if db != nb:
                wait_bank(eng, db)
            qE, qO = qh[0], qh[1]
            kE, kO = kh[0], kh[1]
            qEs = qE[lo:lo + 64, Qc * 128:(Qc + 1) * 128]
            qOs = qO[lo:lo + 64, Qc * 128:(Qc + 1) * 128]
            eng.matmul(pbank[nb][:, 0:16], qEs,
                       pmkP[lo:lo + 64, h * 16:(h + 1) * 16],
                       start=True, stop=False)
            i = eng.matmul(pbank[nb][:, 0:16], qOs,
                           pmkP[lo:lo + 64, 256 + h * 16:256 + (h + 1) * 16],
                           start=False, stop=True)
            for tj in range(Qc + 1):
                if tj < Qc or db == nb:
                    out = pbank[nb][:, 16 + tj * 128:16 + (tj + 1) * 128]
                else:
                    out = pbank[db][:, doff:doff + 128]
                eng.matmul(out, qEs, kE[lo:lo + 64, tj * 128:(tj + 1) * 128],
                           start=True, stop=False)
                i = eng.matmul(out, qOs, kO[lo:lo + 64, tj * 128:(tj + 1) * 128],
                               start=False, stop=True)
            W[f"sc_{w}_{h}_{Qc}"] = minc("pe", i)
            if Qc == 3 and half == 1:
                W[f"qk_consumed_{w}_{hp}"] = ms["pe"]

        def soft_dve(eng, w, h, Qc):
            nb, nw, db, doff = SC_LAYOUT[Qc]
            eng.wait_ge(sems["pe"], W[f"sc_{w}_{h}_{Qc}"])
            diag = pbank[db][:, doff:doff + 128]
            Bt = B0 if Qc == 0 else Btri
            eng.tensor_add(out=sd_sb[:, :], in0=diag, in1=Bt[:, :])
            mx = stat[:, 40:41]; r1 = stat[:, 41:42]
            eng.reduce_max(out=mx, in_=pbank[nb][:, 0:nw], axis=AX.X)
            i = eng.reduce_max(out=r1, in_=sd_sb[:, :], axis=AX.X)
            hop("dve", eng, i)
            i = eng.tensor_max(out=mx, in0=mx, in1=r1)
            hop("dve", eng, i)
            i = eng.tensor_scalar(out=mx, in0=mx, scalar1=-SCALE, scalar2=None,
                                  op0=ALU.mult)
            W[f"mask_{w}_{h}_{Qc}"] = minc("dve", i)

        def soft_act(eng, w, h, Qc):
            nb, nw, db, doff = SC_LAYOUT[Qc]
            eng.wait_ge(sems["dve"], W[f"mask_{w}_{h}_{Qc}"])
            P = P_sb[Qc % 2]
            mx = stat[:, 40:41]
            a1 = stat[:, 44:45]; a2 = stat[:, 45:46]
            eng.activation(out=P[:, 0:nw], in_=pbank[nb][:, 0:nw], func=AF.Exp,
                           scale=SCALE, bias=mx, accum_out=a1)
            i = eng.activation(out=P[:, nw:nw + 128], in_=sd_sb[:, :], func=AF.Exp,
                               scale=SCALE, bias=mx, accum_out=a2)
            W[f"exp_{w}_{h}_{Qc}"] = minc("act", i)
            set_guard(nb, "act", ms["act"])
            if db != nb:
                set_guard(db, "act", ms["act"])

        def soft_norm_dve(eng, w, h, Qc):
            eng.wait_ge(sems["act"], W[f"exp_{w}_{h}_{Qc}"])
            P = P_sb[Qc % 2]
            a1 = stat[:, 44:45]; a2 = stat[:, 45:46]; rs = stat[:, 46:47]
            nw = 16 + (Qc + 1) * 128
            i = eng.tensor_add(out=rs, in0=a1, in1=a2)
            hop("dve", eng, i)
            i = eng.reciprocal(out=rs, in_=rs)
            hop("dve", eng, i)
            i = eng.tensor_scalar(out=P[:, 0:nw], in0=P[:, 0:nw], scalar1=rs,
                                  scalar2=None, op0=ALU.mult)
            W[f"pnorm_{w}_{h}_{Qc}"] = minc("dve", i)

        def pt_pe(eng, w, h, Qc):
            eng.wait_ge(sems["dve"], W[f"pnorm_{w}_{h}_{Qc}"])
            P = P_sb[Qc % 2]
            bk = Qc % 2
            wait_bank(eng, bk)
            eng.matmul(pbank[bk][0:16, 0:128], P[:, 0:16], I128[:, :],
                       start=True, stop=True)
            i = None
            for tj in range(Qc + 1):
                if tj < 3:
                    out = pbank[bk][:, 128 + tj * 128:128 + (tj + 1) * 128]
                else:
                    bk2 = 1 - bk
                    wait_bank(eng, bk2)
                    out = pbank[bk2][:, 0:128]
                i = eng.matmul(out, P[:, 16 + tj * 128:16 + (tj + 1) * 128],
                               I128[:, :], start=True, stop=True)
            W[f"ptmm_{w}_{h}_{Qc}"] = minc("pe", i)

        def pt_evac_act(eng, w, h, Qc):
            bk = Qc % 2
            eng.wait_ge(sems["pe"], W[f"ptmm_{w}_{h}_{Qc}"])
            i = eng.activation(out=pmPT[0:16, Qc * 128:(Qc + 1) * 128],
                               in_=pbank[bk][0:16, 0:128], func=AF.Copy)
            for tj in range(Qc + 1):
                if tj < 3:
                    sp = pbank[bk][:, 128 + tj * 128:128 + (tj + 1) * 128]
                else:
                    sp = pbank[1 - bk][:, 0:128]
                i = eng.activation(out=PT_t[:, tj, Qc * 128:(Qc + 1) * 128],
                                   in_=sp, func=AF.Copy)
            m = minc("act", i)
            W[f"ptev_{w}_{h}_{Qc}"] = m
            set_guard(bk, "act", m)
            if Qc == 3:
                set_guard(1 - bk, "act", m)

        def av_mm(eng, w, h, Qc):
            eng.wait_ge(sems["act"], W[f"ptev_{w}_{h}_{Qc}"])
            if Qc == 0:
                eng.wait_ge(sems["act"], W[f"vev_{w}_{h//2}"])
                if w == 0 and h == 0:
                    eng.wait_ge(sems["d_c2"], W["c2_dmas"])
            bank = 6 + (h % 2)
            if Qc == 0:
                wait_bank(eng, bank)
            out = pbank[bank][:, Qc * 128:(Qc + 1) * 128]
            eng.matmul(out, pmv_t[:, h * 128:(h + 1) * 128],
                       pmPT[:, Qc * 128:(Qc + 1) * 128], start=True, stop=False)
            i = None
            for tj in range(Qc + 1):
                i = eng.matmul(out, v_sb[tj][:, h * 128:(h + 1) * 128],
                               PT_t[:, tj, Qc * 128:(Qc + 1) * 128],
                               start=False, stop=(tj == Qc))
            W[f"av_{w}_{h}_{Qc}"] = minc("pe", i)

        def av_evac_act(eng, w, h):
            bank = 6 + (h % 2)
            eng.wait_ge(sems["pe"], W[f"av_{w}_{h}_3"])
            i = eng.activation(out=attn_sb[:, h, :], in_=pbank[bank][:, :], func=AF.Copy)
            W[f"attnev_{w}_{h}"] = minc("act", i)
            set_guard(bank, "act", ms["act"])

        def wo_load(eng, w, os_):
            par = os_ % 2
            if w == 0 and os_ == 0 and GHOST:
                eng.wait_ge(sems["cc"], 3)
            if os_ < 2:
                eng.wait_ge(sems["pe"], W[f"vuse_{w}_{6+os_}"])
            else:
                eng.wait_ge(sems["pe"], W[f"wouse_{w}_{os_-2}"])
            dinc(f"d_vw{par}", eng.dma_start(
                out=bigw[par][:, :, :],
                in_=woutT_v[:, :, os_ * 256:(os_ + 1) * 256]))
            W[f"wold_{w}_{os_}"] = dma[f"d_vw{par}"]

        def wo_mm(eng, w, os_):
            par = os_ % 2
            eng.wait_ge(sems[f"d_vw{par}"], W[f"wold_{w}_{os_}"])
            if os_ == 0:
                eng.wait_ge(sems["act"], W[f"attnev_{w}_15"])
            for qc in range(4):
                bank = 2 + (os_ * 4 + qc) % 4
                wait_bank(eng, bank)
                for hh in range(16):
                    i = eng.matmul(pbank[bank][:, 0:256],
                                   attn_sb[:, hh, qc * 128:(qc + 1) * 128],
                                   bigw[par][:, hh, :], start=(hh == 0), stop=(hh == 15))
                W[f"womm_{w}_{os_}_{qc}"] = minc("pe", i)
            W[f"wouse_{w}_{os_}"] = ms["pe"]

        def wo_resid_dve(eng, w, os_, qc):
            bank = 2 + (os_ * 4 + qc) % 4
            eng.wait_ge(sems["pe"], W[f"womm_{w}_{os_}_{qc}"])
            i = eng.tensor_add(
                out=x_sb[qc][:, os_ * 256:(os_ + 1) * 256],
                in0=pbank[bank][:, 0:256],
                in1=x_sb[qc][:, os_ * 256:(os_ + 1) * 256])
            W[f"worid_{w}_{os_}_{qc}"] = minc("dve", i)
            set_guard(bank, "dve", ms["dve"])

        def gu_slot(mat, sl):
            if mat == 0:
                return bigw[sl % 2].ap()[:, :, :]
            return qkw.ap()[:, :, :] if sl % 2 == 0 else upslot1

        def ffn_gu_load(eng, w, mat, sl):
            par = mat
            if sl == 0:
                eng.wait_ge(sems["cc"], mat + 1)
            if sl >= 2:
                eng.wait_ge(sems["pe"], W[f"guuse_{w}_{mat}_{sl-2}"])
            elif mat == 0:
                eng.wait_ge(sems["pe"], W[f"wouse_{w}_{6+sl}"])
            elif sl == 0:
                eng.wait_ge(sems["pe"], W[f"qkuse_{w}_{31}"])
            else:
                eng.wait_ge(sems["dve"], W[f"rope_{w}_7"])
            wd = min(256, INTER - sl * 256)
            src = (WgT_v if mat == 0 else WuT_v)[:, :, sl * 256:sl * 256 + wd]
            dinc(f"d_g{par}", eng.dma_start(out=gu_slot(mat, sl)[:, :, 0:wd], in_=src))
            W[f"guld_{w}_{mat}_{sl}"] = dma[f"d_g{par}"]

        def ffn_gu_mm(eng, w, mat, sl):
            par = mat
            eng.wait_ge(sems[f"d_g{par}"], W[f"guld_{w}_{mat}_{sl}"])
            if mat == 0 and sl == 0:
                eng.wait_ge(sems["act"], W[f"xT_{w}_3"])
            slot = gu_slot(mat, sl)
            i = None
            for il in range(min(2, NIC - sl * 2)):
                ic = sl * 2 + il
                bank = (2 if mat == 0 else 3) + 2 * (ic % 2)
                wait_bank(eng, bank)
                for dc in range(16):
                    i = eng.matmul(pbank[bank][:, :],
                                   slot[:, dc, il * 128:(il + 1) * 128],
                                   xnT[:, dc, :], start=(dc == 0), stop=(dc == 15))
                W[f"gumm_{w}_{mat}_{ic}"] = minc("pe", i)
            W[f"guuse_{w}_{mat}_{sl}"] = ms["pe"]

        def ffn_silu_act(eng, w, ic):
            bank = 2 + 2 * (ic % 2)
            eng.wait_ge(sems["pe"], W[f"gumm_{w}_0_{ic}"])
            if ic > 0:
                eng.wait_ge(sems["dve"], W[f"h_{w}_{ic-1}"])
            i = eng.activation(out=silu_sb[:, :], in_=pbank[bank][:, :], func=AF.Silu)
            W[f"silu_{w}_{ic}"] = minc("act", i)
            set_guard(bank, "act", ms["act"])

        def ffn_h_dve(eng, w, ic):
            bank = 3 + 2 * (ic % 2)
            eng.wait_ge(sems["act"], W[f"silu_{w}_{ic}"])
            eng.wait_ge(sems["pe"], W[f"gumm_{w}_1_{ic}"])
            i = eng.tensor_mul(out=h_sb[:, ic, :], in0=pbank[bank][:, :],
                               in1=silu_sb[:, :])
            W[f"h_{w}_{ic}"] = minc("dve", i)
            set_guard(bank, "dve", ms["dve"])

        def ffn_dw_load(eng, w, oh, sl):
            idx = oh * 11 + sl
            par = sl % 2
            if w == 0 and idx == 0:
                eng.wait_ge(sems["cc"], 3)
            prev = W.get(f"dwprev_{w}_{par}")
            if prev is not None:
                eng.wait_ge(sems["pe"], prev)
            else:
                eng.wait_ge(sems["pe"], W[f"guuse_{w}_{par}_21"])
            nic = min(4, NIC - sl * 4)
            src = WdT_v[:, sl * 4: sl * 4 + nic, oh * 1024:(oh + 1) * 1024]
            dinc(f"d_g{par}", eng.dma_start(out=bigw_d[par][:, 0:nic, :], in_=src))
            W[f"dwld_{w}_{oh}_{sl}"] = dma[f"d_g{par}"]

        def ffn_dw_mm(eng, w, oh, sl):
            par = sl % 2
            eng.wait_ge(sems[f"d_g{par}"], W[f"dwld_{w}_{oh}_{sl}"])
            nic = min(4, NIC - sl * 4)
            i = None
            for il in range(nic):
                ic = sl * 4 + il
                if oh == 0:
                    eng.wait_ge(sems["dve"], W[f"h_{w}_{ic}"])
                if ic == 0:
                    for bk in range(8):
                        wait_bank(eng, bk)
                for tc in range(4):
                    for os_ in range(2):
                        bank = tc * 2 + os_
                        i = eng.matmul(pbank[bank][:, :],
                                       h_sb[:, ic, tc * 128:(tc + 1) * 128],
                                       bigw_d[par][:, il, os_ * 512:(os_ + 1) * 512],
                                       start=(ic == 0), stop=(ic == NIC - 1))
            W[f"dwuse_{w}_{oh}_{sl}"] = minc("pe", i)
            W[f"dwprev_{w}_{sl % 2}"] = ms["pe"]
            if sl == 10:
                W[f"dwdone_{w}_{oh}"] = ms["pe"]

        def ffn_dw_resid_dve(eng, w, oh):
            eng.wait_ge(sems["pe"], W[f"dwdone_{w}_{oh}"])
            for tc in range(4):
                for os_ in range(2):
                    bank = tc * 2 + os_
                    i = eng.tensor_add(
                        out=x_sb[tc][:, oh * 1024 + os_ * 512: oh * 1024 + (os_ + 1) * 512],
                        in0=pbank[bank][:, :],
                        in1=x_sb[tc][:, oh * 1024 + os_ * 512: oh * 1024 + (os_ + 1) * 512])
                    set_guard(bank, "dve", minc("dve", i))
                if oh == 1:
                    W[f"xdone_{w}_{tc}"] = ms["dve"]

        def x_store(eng, w, tc):
            eng.wait_ge(sems["dve"], W[f"xdone_{w}_{tc}"])
            dinc("d_out", eng.dma_start(
                out=d_y[w * TOK + tc * 128: w * TOK + (tc + 1) * 128, :],
                in_=x_sb[tc][:, :]))
            W[f"store_{w}_{tc}"] = dma["d_out"]

        def dbg_dump(items):
            pe_v, dve_v, act_v, pool_v = ms["pe"], ms["dve"], ms["act"], ms["pool"]
            def go(eng):
                eng.wait_ge(sems["pe"], pe_v)
                eng.wait_ge(sems["dve"], dve_v)
                eng.wait_ge(sems["act"], act_v)
                eng.wait_ge(sems["pool"], pool_v)
                for ap_, row in items:
                    rows = ap_.shape[0]
                    dinc("d_out", eng.dma_start(out=d_y[row:row + rows, 0:ap_.shape[-1]],
                                                in_=ap_))
            block.gpsimd(go)

        for w in range(NWIN):
            for tc in range(4):
                block.scalar(lambda e, w=w, tc=tc: x_load(e, w, tc))
            block.gpsimd(lambda e, w=w: rope_build_pool(e, w))
            block.vector(lambda e, w=w: rope_build_dve(e, w))
            block.scalar(lambda e, w=w: rope_build_act(e, w))

            def ln1_stats_all(eng, w=w):
                i = None
                for tc in range(4):
                    i = ln_stats_act(eng, w, tc, "ln1")
                W[f"ln1st_{w}"] = minc("act", i)
            block.scalar(ln1_stats_all)
            block.vector(lambda e, w=w: ln_var_dve(e, w, "ln1"))
            block.scalar(lambda e, w=w: ln_rstd_act(e, w, "ln1"))
            block.vector(lambda e, w=w: ln1_rms_dve(e, w))
            block.scalar(lambda e, w=w: ln1_rms_act(e, w))
            block.vector(lambda e, w=w: ln1_ab_dve(e, w))
            for tc in range(4):
                block.vector(lambda e, w=w, tc=tc: xstage_dve(e, w, tc, "xn"))
                for grp in range(4):
                    block.tensor(lambda e, w=w, tc=tc, grp=grp: xT_pe_grp(e, w, tc, grp))
                    block.scalar(lambda e, w=w, tc=tc, grp=grp: xT_evac_grp(e, w, tc, grp))
                if w == 0:
                    ghost_tick(2)
            if STAGE <= 1:
                ghost_tick(n_ghost)
                dbg_dump([(xnT.ap().rearrange("p a b -> p (a b)")[:, j * 2048:(j + 1) * 2048],
                           j * 128) for j in range(4)]
                         + [(x_sb[j].ap()[:, :], 512 + j * 128) for j in range(4)]
                         + [(stat.ap()[:, :], 1024)])
                continue

            for hp in range(8):
                for which in range(4):
                    block.gpsimd(lambda e, w=w, hp=hp, which=which:
                                 qk_wload(e, w, hp, which))
                    block.tensor(lambda e, w=w, hp=hp, which=which:
                                 qk_mm(e, w, hp, which))
                block.vector(lambda e, w=w, hp=hp: rope_pair_dve(e, w, hp))
                block.gpsimd(lambda e, w=w, vs=hp: v_wload(e, w, vs))
                for tc in range(4):
                    block.tensor(lambda e, w=w, vs=hp, tc=tc: v_mm_tc(e, w, vs, tc))
                    block.scalar(lambda e, w=w, vs=hp, tc=tc: v_evac_tc(e, w, vs, tc))
                if w == 0:
                    ghost_tick(3)
                if STAGE <= 2:
                    continue

                for h in (2 * hp, 2 * hp + 1):
                    def chain(Qc, w=w, h=h):
                        block.tensor(lambda e, w=w, h=h, Qc=Qc: sc_mm(e, w, h, Qc))
                        block.vector(lambda e, w=w, h=h, Qc=Qc: soft_dve(e, w, h, Qc))
                        block.scalar(lambda e, w=w, h=h, Qc=Qc: soft_act(e, w, h, Qc))
                        block.vector(lambda e, w=w, h=h, Qc=Qc:
                                     soft_norm_dve(e, w, h, Qc))
                    def ptav(Qc, w=w, h=h):
                        block.tensor(lambda e, w=w, h=h, Qc=Qc: pt_pe(e, w, h, Qc))
                        block.scalar(lambda e, w=w, h=h, Qc=Qc:
                                     pt_evac_act(e, w, h, Qc))
                        block.tensor(lambda e, w=w, h=h, Qc=Qc: av_mm(e, w, h, Qc))
                    chain(0)
                    chain(1)
                    ptav(0)
                    chain(2)
                    ptav(1)
                    chain(3)
                    ptav(2)
                    ptav(3)
                    block.scalar(lambda e, w=w, h=h: av_evac_act(e, w, h))
                    if w == 0:
                        ghost_tick(1)

            if w == 0 and GHOST:
                ghost_tick(n_ghost)
                for mat in range(3):
                    block.gpsimd(lambda e, mat=mat: ghost_collective(e, mat))
            if STAGE <= 3:
                dbg_dump([(attn_sb.ap().rearrange("p a b -> p (a b)")[:, j * 2048:(j + 1) * 2048],
                           j * 128) for j in range(4)]
                         + [(P_sb[1].ap()[:, 0:544], 512)]
                         + [(qh[0].ap()[:, :], 640), (kh[0].ap()[:, :], 768)])
                continue

            n_os = 8 if STAGE < 30 or STAGE >= 40 else (STAGE - 30)
            for os_ in range(n_os):
                block.gpsimd(lambda e, w=w, os_=os_: wo_load(e, w, os_))
                block.tensor(lambda e, w=w, os_=os_: wo_mm(e, w, os_))
                for qc in range(4):
                    block.vector(lambda e, w=w, os_=os_, qc=qc:
                                 wo_resid_dve(e, w, os_, qc))

            if STAGE <= 4 or (30 <= STAGE < 40):
                if STAGE == 4 or STAGE == 38:
                    dbg_dump([(x_sb[j].ap()[:, :], j * 128) for j in range(4)])
                continue

            def ln2_stats_all(eng, w=w):
                i = None
                for tc in range(4):
                    i = ln_stats_act(eng, w, tc, "ln2")
                W[f"ln2st_{w}"] = minc("act", i)
            block.scalar(ln2_stats_all)
            block.vector(lambda e, w=w: ln_var_dve(e, w, "ln2"))
            block.scalar(lambda e, w=w: ln_rstd_act(e, w, "ln2"))
            block.vector(lambda e, w=w: ln2_ab_dve(e, w))
            for tc in range(4):
                block.vector(lambda e, w=w, tc=tc: xstage_dve(e, w, tc, "x2"))
                for grp in range(4):
                    block.tensor(lambda e, w=w, tc=tc, grp=grp: xT_pe_grp(e, w, tc, grp))
                    block.scalar(lambda e, w=w, tc=tc, grp=grp: xT_evac_grp(e, w, tc, grp))

            if STAGE <= 5:
                dbg_dump([(xnT.ap().rearrange("p a b -> p (a b)")[:, j * 2048:(j + 1) * 2048],
                           j * 128) for j in range(4)]
                         + [(x_sb[j].ap()[:, :], 512 + j * 128) for j in range(4)]
                         + [(stat.ap()[:, :], 1024)])
                continue
            for sl in range(22):
                for mat in range(2):
                    block.sync(lambda e, w=w, mat=mat, sl=sl: ffn_gu_load(e, w, mat, sl))
                for mat in range(2):
                    block.tensor(lambda e, w=w, mat=mat, sl=sl: ffn_gu_mm(e, w, mat, sl))
                for il in range(min(2, NIC - sl * 2)):
                    ic = sl * 2 + il
                    block.scalar(lambda e, w=w, ic=ic: ffn_silu_act(e, w, ic))
                    block.vector(lambda e, w=w, ic=ic: ffn_h_dve(e, w, ic))

            if STAGE <= 6:
                dbg_dump([(h_sb.ap().rearrange("p a b -> p (a b)")[:, j * 2048:(j + 1) * 2048],
                           j * 128) for j in range(8)])
                continue
            for oh in range(2):
                for sl in range(11):
                    block.sync(lambda e, w=w, oh=oh, sl=sl: ffn_dw_load(e, w, oh, sl))
                    block.tensor(lambda e, w=w, oh=oh, sl=sl: ffn_dw_mm(e, w, oh, sl))
                block.vector(lambda e, w=w, oh=oh: ffn_dw_resid_dve(e, w, oh))
            W[f"bigw_free_{w+1}_0"] = W[f"dwprev_{w}_0"]
            W[f"bigw_free_{w+1}_1"] = W[f"dwprev_{w}_1"]
            for tc in range(4):
                block.sync(lambda e, w=w, tc=tc: x_store(e, w, tc))

        def final_wait(eng):
            if GHOST and ms["cc"] >= 3:
                eng.wait_ge(sems["cc"], 3)
            eng.wait_ge(sems["d_out"], dma["d_out"])
        block.sync(final_wait)

        real_block = ctx.enter_context(nc.Block())
        real_block.sync(block.recs["sync"].replay)
        real_block.vector(block.recs["vector"].replay)
        real_block.scalar(block.recs["scalar"].replay)
        real_block.tensor(block.recs["tensor"].replay)
        real_block.gpsimd(block.recs["gpsimd"].replay)

    return nc


def _host_prep(L0_attn_x, L0_attn_norm_w, L0_attn_w_qkv, L0_attn_w_out,
               L0_attn_pm, L0_ffn_lut, L0_ffn_gate_idx, L0_ffn_up_idx,
               L0_ffn_down_idx):
    perm_eo = np.concatenate([np.arange(0, 128, 2), np.arange(1, 128, 2)])
    order = []
    for part in range(2):
        base = part * 2048
        for hp in range(8):
            A, B = 2 * hp, 2 * hp + 1
            order += [base + A * 128 + j for j in perm_eo[:64]]
            order += [base + B * 128 + j for j in perm_eo[:64]]
            order += [base + A * 128 + j for j in perm_eo[64:]]
            order += [base + B * 128 + j for j in perm_eo[64:]]
    order += list(range(4096, 6144))
    wqkvT = np.ascontiguousarray(L0_attn_w_qkv[order, :].T)
    woutT = np.ascontiguousarray(L0_attn_w_out.T)
    pmkE = L0_attn_pm[0][:, :, 0::2].transpose(2, 0, 1).reshape(64, 256)
    pmkO = L0_attn_pm[0][:, :, 1::2].transpose(2, 0, 1).reshape(64, 256)
    half = np.concatenate([pmkE, pmkO], axis=1)
    pmk = np.ascontiguousarray(np.concatenate([half, half], axis=0))
    pmv = np.ascontiguousarray(L0_attn_pm[1].transpose(1, 0, 2).reshape(16, 2048))
    gidxT = np.ascontiguousarray(L0_ffn_gate_idx.T)
    uidxT = np.ascontiguousarray(L0_ffn_up_idx.T)
    didxT = np.ascontiguousarray(L0_ffn_down_idx.T)
    common = {
        "wqkvT": wqkvT, "woutT": woutT, "pmk": pmk, "pmv": pmv,
        "normw": np.ascontiguousarray(L0_attn_norm_w.reshape(1, D)),
        "lut": np.ascontiguousarray(L0_ffn_lut.reshape(1, 16)),
    }
    in_maps = []
    for c in range(NCORE):
        m = dict(common)
        m["x"] = np.ascontiguousarray(L0_attn_x[c])
        m["gidx"] = np.ascontiguousarray(gidxT[c * GSH:(c + 1) * GSH])
        m["uidx"] = np.ascontiguousarray(uidxT[c * GSH:(c + 1) * GSH])
        m["didx"] = np.ascontiguousarray(didxT[c * DSH:(c + 1) * DSH])
        in_maps.append(m)
    return in_maps


def kernel(L0_attn_x, L0_attn_norm_w, L0_attn_w_qkv, L0_attn_w_out, L0_attn_pm,
           L0_ffn_lut, L0_ffn_gate_idx, L0_ffn_up_idx, L0_ffn_down_idx):
    in_maps = _host_prep(np.asarray(L0_attn_x), np.asarray(L0_attn_norm_w),
                         np.asarray(L0_attn_w_qkv), np.asarray(L0_attn_w_out),
                         np.asarray(L0_attn_pm), np.asarray(L0_ffn_lut),
                         np.asarray(L0_ffn_gate_idx), np.asarray(L0_ffn_up_idx),
                         np.asarray(L0_ffn_down_idx))
    nc = build_program()
    res = run_bass_kernel_spmd(nc, in_maps, core_ids=list(range(NCORE)))
    return np.stack([res.results[c]["y"] for c in range(NCORE)], axis=0)
